# version 09
# baseline (speedup 1.0000x reference)
import sys
sys.path.insert(0, "/opt/trn_rl_repo")
"""Megakernel builder for mk_sandwich_d1.

Sharding (uniform SPMD program, 8 cores):
  core c: batch b=c//2, half h=c%2, owns tokens [1024h:1024h+1024],
  heads [16h:16h+16], kv heads [4h:4h+4].
Build is two-pass: pass 1 runs all emitters against mock engines to
compute every semaphore milestone; pass 2 emits for real.
"""
import math
from collections import defaultdict

import numpy as np
import ml_dtypes

import concourse.bass as bass
import concourse.mybir as mybir
from concourse.bass import IndirectOffsetOnAxis

F32 = mybir.dt.float32
BF16 = mybir.dt.bfloat16
I32 = mybir.dt.int32
AF = mybir.ActivationFunctionType
OP = mybir.AluOpType
AXX = mybir.AxisListType.X

B, S, D = 4, 2048, 4096
HD, NQ, NKV = 128, 32, 8
HQC, HKVC = 16, 4
TOK = 1024
NMC = HQC + HKVC
KC = 32
NVC = HKVC * HD
SCALE = 1.0 / math.sqrt(HD)
NEG = -1.0e30
R_LORA, N_EXP = 16, 8
LORA_SCALING = 2.0
NQT = 4

BF = np.dtype(ml_dtypes.bfloat16)

PHASES = ["p2", "p3", "p4", "p5", "p6"]


def _wtiles(w_t):
    K, M = w_t.shape
    a = w_t.reshape(KC, 128, M // 128, 128)
    a = a.transpose(2, 1, 0, 3)
    return np.ascontiguousarray(a)


def make_core_inputs(inputs, c):
    b, h = c // 2, c % 2
    x = np.asarray(inputs["L0_attn_x"], dtype=np.float32)
    w_qkv = np.asarray(inputs["L0_attn_w_qkv"], dtype=np.float32)
    b_qkv = np.asarray(inputs["L0_attn_b_qkv"], dtype=np.float32)
    w_o = np.asarray(inputs["L0_attn_w_o"], dtype=np.float32)
    b_o = np.asarray(inputs["L0_attn_b_o"], dtype=np.float32)
    gate_w = np.asarray(inputs["L0_ffn_gate_w"], dtype=np.float32)
    base_w = np.asarray(inputs["L0_ffn_base_w"], dtype=np.float32)
    base_b = np.asarray(inputs["L0_ffn_base_b"], dtype=np.float32)
    lora_A = np.asarray(inputs["L0_ffn_lora_A"], dtype=np.float32)
    lora_B = np.asarray(inputs["L0_ffn_lora_B"], dtype=np.float32)

    d = {}
    d["x"] = np.ascontiguousarray(x[b])
    d["xo"] = np.ascontiguousarray(x[b, h * TOK:(h + 1) * TOK])

    qrows = w_qkv[h * 2048:(h + 1) * 2048]
    krows = w_qkv[4096 + h * 512:4096 + (h + 1) * 512]
    wqk = np.concatenate([qrows, krows], axis=0)
    d["wqk"] = _wtiles(np.ascontiguousarray(wqk.T))
    bq = b_qkv[h * 2048:(h + 1) * 2048]
    bk = b_qkv[4096 + h * 512:4096 + (h + 1) * 512]
    d["bqk"] = np.ascontiguousarray(
        np.concatenate([bq, bk]).reshape(NMC, 128).T)

    vrows = w_qkv[5120 + h * 512:5120 + (h + 1) * 512]
    wv = vrows.T.reshape(KC, 128, NVC).transpose(1, 0, 2)
    d["wv"] = np.ascontiguousarray(wv)
    bv = b_qkv[5120 + h * 512:5120 + (h + 1) * 512]
    bvf = np.repeat(bv.reshape(HKVC, 128), 4, axis=0)
    d["bv"] = np.ascontiguousarray(bvf.T)

    my = np.arange(h * 16, h * 16 + 16)
    other = np.arange((1 - h) * 16, (1 - h) * 16 + 16)
    dorder = np.concatenate([my, other])
    wot = w_o.T.reshape(32, 128, 4096)[dorder].reshape(4096, 4096)
    d["wo"] = _wtiles(np.ascontiguousarray(wot))
    d["bo"] = np.ascontiguousarray(b_o.reshape(32, 128).T)

    d["bw"] = _wtiles(np.ascontiguousarray(base_w.T))
    d["bb"] = np.ascontiguousarray(base_b.reshape(32, 128).T)
    A_cat = lora_A.reshape(N_EXP * R_LORA, D)
    d["la"] = np.ascontiguousarray(
        A_cat.T.reshape(KC, 128, 128).transpose(1, 0, 2))
    d["lb"] = np.ascontiguousarray(
        lora_B.transpose(0, 2, 1).reshape(N_EXP * R_LORA, D))
    d["gw"] = np.ascontiguousarray(
        gate_w.T.reshape(KC, 128, N_EXP).transpose(1, 0, 2))
    rexp = np.zeros((N_EXP, 128), dtype=np.float32)
    for e in range(N_EXP):
        rexp[e, e * R_LORA:(e + 1) * R_LORA] = LORA_SCALING
    d["rexp"] = rexp.astype(BF)

    inv = 1.0 / (10000.0 ** (np.arange(0, HD, 2, dtype=np.float64) / HD))
    ang = np.arange(S, dtype=np.float64)[:, None] * inv[None, :]
    cos = np.cos(ang).T
    sin = np.sin(ang).T
    rc = np.repeat(cos, 2, axis=0).astype(np.float32)
    rs = np.empty((HD, S), dtype=np.float32)
    rs[0::2] = -sin
    rs[1::2] = sin
    d["rc"] = rc.astype(BF)
    d["rs"] = rs.astype(BF)

    perm = np.zeros((128, 128), dtype=np.float32)
    for k in range(128):
        perm[k, k ^ 1] = 1.0
    d["perm"] = perm.astype(BF)
    d["ident"] = np.eye(128, dtype=np.float32).astype(BF)

    cm = np.zeros((128, 128), dtype=np.float32)
    cm[np.triu_indices(128, 1)] = NEG
    d["cmask"] = cm

    partner = c ^ 1
    d["pidx"] = np.ascontiguousarray(
        (partner * 2048 + np.arange(2048, dtype=np.int32)).reshape(16, 128).T)
    d["selA"] = np.full((128, 1), 1.0 - h, dtype=np.float32)
    d["selB"] = np.full((128, 1), float(h), dtype=np.float32)
    return d


def _ap_span(ap):
    try:
        dims = list(ap.ap)
    except Exception:
        return None
    off = ap.offset if isinstance(ap.offset, int) else 0
    span = 1
    for st, ct in dims[1:]:
        span += (ct - 1) * st
    return (ap.tensor.name, off, off + span)


def _overlap(a, b):
    return a and b and a[0] == b[0] and a[1] < b[2] and b[1] < a[2]


class EngGuard:

    DATA_OPS = {"tensor_copy", "tensor_tensor", "tensor_add", "tensor_sub",
                "tensor_mul", "tensor_max", "tensor_scalar",
                "tensor_single_scalar", "tensor_scalar_add",
                "tensor_scalar_mul", "tensor_reduce", "reduce_max",
                "reduce_sum", "scalar_tensor_tensor", "reciprocal",
                "select", "activation", "copy", "add", "mul", "sign",
                "square", "sqrt", "memset",
                "dma_start", "dma_start_transpose"}

    def __init__(self, eng, junk_out, junk_in):
        self._eng = eng
        self._last_outs = []
        self._junk_out = junk_out
        self._junk_in = junk_in
        self._gaps = 0

    def _gap(self):
        self._eng.tensor_copy(self._junk_out, self._junk_in)             if hasattr(self._eng, "tensor_copy")             else self._eng.copy(self._junk_out, self._junk_in)
        self._gaps += 1

    def __getattr__(self, name):
        real = getattr(self._eng, name)
        if name not in self.DATA_OPS:
            return real

        def call(*args, **kw):
            from concourse.bass import AP as _AP
            aps = [a for a in args if isinstance(a, _AP)]
            for v in kw.values():
                if isinstance(v, _AP):
                    aps.append(v)
            spans = [_ap_span(a) for a in aps]
            if any(_overlap(lo, sp) for lo in self._last_outs
                   for sp in spans):
                self._gap()
            res = real(*args, **kw)
            outs = []
            if name in ("dma_start", "dma_start_transpose"):
                return res
            if args and isinstance(args[0], _AP):
                outs.append(_ap_span(args[0]))
            ao = kw.get("accum_out")
            if isinstance(ao, _AP):
                outs.append(_ap_span(ao))
            self._last_outs = [o for o in outs if o]
            return res
        return call


class _MockInst:
    def then_inc(self, sem, n=1):
        return self


class _MockEng:
    def __getattr__(self, name):
        def f(*a, **k):
            return _MockInst()
        return f


class Sem:
    def __init__(self, handle, name):
        self.h, self.name, self.v = handle, name, 0

    def inc(self, inst, n=1):
        self.v += n
        inst.then_inc(self.h, n)
        return self.v


class MK:
    def __init__(self, upto="p6", dumps=(), detect_races=False):
        self.upto = upto
        self.dumps = set(dumps)
        self.nc = bass.Bass("TRN2", target_bir_lowering=False,
                            detect_race_conditions=detect_races)
        self.streams = defaultdict(list)
        self.ev = {}
        self.wanted = set()
        self._declare_io()
        self._alloc_sbuf()
        self._alloc_psum()

    def emit(self, eng_name):
        def deco(fn):
            self.streams[eng_name].append(fn)
        return deco

    def mark(self, name, sem, val):
        if self.pass1:
            assert name not in self.ev, f"duplicate event {name}"
            self.ev[name] = (sem, val)
        else:
            assert self.ev[name] == (sem, val), (name, self.ev[name], val)

    def wait(self, e, ename, name):
        if self.pass1:
            self.wanted.add(name)
            return
        sem, val = self.ev[name]
        if val <= 0:
            return
        key = (ename, sem.name)
        if self.last_wait.get(key, -1) >= val:
            return
        e.wait_ge(sem.h, val)
        self.last_wait[key] = val

    def wait_val(self, e, ename, sem, val):
        if self.pass1 or val <= 0:
            return
        key = (ename, sem.name)
        if self.last_wait.get(key, -1) >= val:
            return
        e.wait_ge(sem.h, val)
        self.last_wait[key] = val

    def dmae(self, e, ename, lane, out, in_, mark=None, transpose=False,
             selfgate=True):
        s = self.sems[lane]
        if selfgate:
            self.wait_val(e, ename, s, s.v)
        if transpose:
            ins = e.dma_start_transpose(out=out, in_=in_)
        else:
            ins = e.dma_start(out=out, in_=in_)
        v = s.inc(ins, 16)
        if mark:
            self.mark(mark, s, v)
        return v

    def _on(self, ph):
        return PHASES.index(self.upto) >= PHASES.index(ph)

    def finalize(self):
        import contextlib
        self.stack = contextlib.ExitStack()
        self.sems = {}
        for name in ["sW0", "sW1", "sW2", "sW3", "sW4", "sXa", "sXb",
                     "sT0", "sT1", "sTp0", "sTp1", "sTo0", "sTo1",
                     "sSt0", "sSt1", "sSt2", "sCC", "sPE", "sACT", "sDVE"]:
            self.sems[name] = Sem(
                self.stack.enter_context(self.nc.semaphore(name)), name)
        self.__dict__.update(self.sems)

        self._build_phases()

        self.pass1 = True
        self.last_wait = {}
        mocks = {k: _MockEng() for k in
                 ["sync", "scalar", "vector", "tensor", "gpsimd"]}
        for ename, fns in self.streams.items():
            for fn in fns:
                fn(mocks[ename], ename)
        missing = self.wanted - set(self.ev)
        assert not missing, f"unknown events waited on: {sorted(missing)[:10]}"
        snap = {}
        for s in self.sems.values():
            snap[s.name] = s.v
            s.v = 0
        self.pass1 = False
        self.last_wait = {}
        self.ev2 = dict(self.ev)
        with self.nc.Block() as block:
            self.guards = {}
            for ename in ["sync", "scalar", "vector", "tensor", "gpsimd"]:
                def runner(e, ename=ename):
                    if ename == "vector":
                        e = EngGuard(e, self.junk[:, 2:3], self.junk[:, 3:4])
                        self.guards[ename] = e
                    elif ename == "scalar":
                        e = EngGuard(e, self.junk[:, 4:5], self.junk[:, 5:6])
                        self.guards[ename] = e
                    for fn in self.streams[ename]:
                        fn(e, ename)
                getattr(block, ename)(runner)
        for en, g in self.guards.items():
            print(f"guard[{en}]: {g._gaps} gap ops inserted")
        for s in self.sems.values():
            assert s.v == snap[s.name], (s.name, s.v, snap[s.name])
        self.stack.close()
        return self.nc

    def _declare_io(self):
        nc = self.nc
        EI = dict(kind="ExternalInput")
        self.hx = nc.dram_tensor("x", [S, D], F32, **EI)
        self.hxo = nc.dram_tensor("xo", [TOK, D], F32, **EI)
        self.hwqk = nc.dram_tensor("wqk", [NMC, 128, KC, 128], F32, **EI)
        self.hbqk = nc.dram_tensor("bqk", [128, NMC], F32, **EI)
        self.hwv = nc.dram_tensor("wv", [128, KC, NVC], F32, **EI)
        self.hbv = nc.dram_tensor("bv", [128, HQC], F32, **EI)
        self.hwo = nc.dram_tensor("wo", [32, 128, 32, 128], F32, **EI)
        self.hbo = nc.dram_tensor("bo", [128, 32], F32, **EI)
        self.hbw = nc.dram_tensor("bw", [32, 128, 32, 128], F32, **EI)
        self.hbb = nc.dram_tensor("bb", [128, 32], F32, **EI)
        self.hla = nc.dram_tensor("la", [128, KC, 128], F32, **EI)
        self.hlb = nc.dram_tensor("lb", [128, D], F32, **EI)
        self.hgw = nc.dram_tensor("gw", [128, KC, N_EXP], F32, **EI)
        self.hrexp = nc.dram_tensor("rexp", [N_EXP, 128], BF16, **EI)
        self.hrc = nc.dram_tensor("rc", [128, S], BF16, **EI)
        self.hrs = nc.dram_tensor("rs", [128, S], BF16, **EI)
        self.hperm = nc.dram_tensor("perm", [128, 128], BF16, **EI)
        self.hident = nc.dram_tensor("ident", [128, 128], BF16, **EI)
        self.hcmask = nc.dram_tensor("cmask", [128, 128], F32, **EI)
        self.hpidx = nc.dram_tensor("pidx", [128, 16], I32, **EI)
        self.hselA = nc.dram_tensor("selA", [128, 1], F32, **EI)
        self.hselB = nc.dram_tensor("selB", [128, 1], F32, **EI)
        self.hout = nc.dram_tensor("out", [TOK, D], F32, kind="ExternalOutput")
        self.cc_in = nc.dram_tensor("cc_in", [HQC * 128, TOK], BF16)
        self.cc_out = nc.dram_tensor("cc_out", [8 * HQC * 128, TOK], BF16)
        self.dump_t = {}

    def dumper(self, name, shape, dtype=F32):
        t = self.nc.dram_tensor("d_" + name, shape, dtype,
                                kind="ExternalOutput")
        self.dump_t[name] = t
        return t

    def _alloc_sbuf(self):
        nc = self.nc
        self.arena = nc.alloc_sbuf_tensor("arena", [128, 212448],
                                          mybir.dt.uint8)
        def at(name, shape, dtype, off):
            return nc.alloc_sbuf_tensor_at(name, shape, dtype, offset=off)
        SB = 16512
        KA, KB, KCr, KD, KE, KF = (SB + 0, SB + 65536, SB + 81920,
                                   SB + 98304, SB + 131072, SB + 196608)
        self.qo = [at(f"qo{i}", [128, 2048], BF16, KA + i * 4096)
                   for i in range(16)]
        self.wosl = [at(f"wosl{i}", [128, KC * 128], BF16, KA + i * 8192)
                     for i in range(2)]
        self.u_tm = [at(f"utm{i}", [128, 4096], BF16, KA + i * 8192)
                     for i in range(8)]
        self.Ksb = [at(f"K{i}", [128, 2048], BF16, KB + i * 4096)
                    for i in range(4)]
        self.la_sb = at("la_sb", [128, KC * 128], BF16, KB)
        self.lb_sb = at("lb_sb", [128, D], BF16, KB + 8192)
        self.v_sb = at("v_sb", [128, 16 * NVC], BF16, KCr)
        self.ffn_tm = [at(f"ffntm{i}", [128, 4096], BF16, KCr + i * 8192)
                       for i in range(2)]
        self.o_oth = [at(f"ooth{i}", [128, 1024], BF16, KB + i * 2048)
                      for i in range(16)]
        self.xln0 = at("xln0", [128, KC * 512], BF16, KD)
        self.o_sel = [at(f"osel{i}", [128, 1024], BF16, KD + i * 2048)
                      for i in range(16)]
        self.x1buf = at("x1buf", [128, 4096], F32, KD)
        self.xobuf = at("xobuf", [128, 4096], F32, KD + 16384)
        self.bwsl = [at(f"bwsl{i}", [128, KC * 128], BF16, KD + i * 16384)
                     for i in range(2)]
        self.xt = at("xt", [128, 4096], F32, KE)
        self.xln_tm = at("xln_tm", [128, 4096], BF16, KE + 16384)
        self.wqksl = [at(f"wqksl{i}", [128, KC * 128], BF16,
                         KE + 24576 + i * 8192) for i in range(2)]
        self.xln1a = at("xln1a", [128, 24 * 512], BF16, KE + 40960)
        self.u_all = at("u_all", [128, 8 * 32 * 128], BF16, KE)
        self.xln3_sb = at("xln3_sb", [128, KC * 512], BF16, KE)
        self.ffn_all = at("ffn_all", [128, 4 * 32 * 128], BF16, KE + 32768)
        self.rc_sb = at("rc_sb", [128, S], BF16, KE)
        self.rs_sb = at("rs_sb", [128, S], BF16, KE + 4096)
        self.p_scr = [at(f"p{i}", [128, 1024], BF16, KE + 8192 + i * 2048)
                      for i in range(2)]
        self.pt_scr = [at(f"pt{i}", [128, 1024], BF16, KE + 12288 + i * 2048)
                       for i in range(2)]
        self.cm_sb = at("cm_sb", [128, 128], F32, KE + 16384)
        self.perm_sb = at("perm_sb", [128, 128], BF16, KE + 16896)
        self.otmp = [at(f"otmp{i}", [128, 128], BF16, KE + 17152 + i * 256)
                     for i in range(2)]
        self.t1_scr = [at(f"t1s{i}", [128, 512], BF16, KE + 17664 + i * 1024)
                       for i in range(2)]
        self.xln1b = at("xln1b", [128, 8 * 512], BF16, KF)
        self.xstage = at("xstage", [128, 4096], BF16, KF)
        FO = KF + 8192
        def nxt(name, shape, dtype, size):
            nonlocal FO
            t = at(name, shape, dtype, FO)
            FO += size
            return t
        self.bqk_sb = nxt("bqk_sb", [128, NMC], F32, 96)
        self.bv_sb = nxt("bv_sb", [128, HQC], F32, 64)
        self.bo_sb = nxt("bo_sb", [128, 32], F32, 128)
        self.bb_sb = nxt("bb_sb", [128, 32], F32, 128)
        self.ident_sb = nxt("ident_sb", [128, 128], BF16, 256)
        self.gw_sb = nxt("gw_sb", [128, KC * N_EXP], BF16, 512)
        self.rexp_sb = nxt("rexp_sb", [N_EXP, 128], BF16, 256)
        self.pidx_sb = nxt("pidx_sb", [128, 16], I32, 64)
        self.selA_sb = nxt("selA_sb", [128, 1], F32, 32)
        self.selB_sb = nxt("selB_sb", [128, 1], F32, 32)
        self.wvsl = [nxt(f"wvsl{i}", [128, NVC], BF16, 1024) for i in range(2)]
        self.ast = nxt("ast", [128, 32], F32, 128)
        self.lnst = nxt("lnst", [128, 64], F32, 256)
        self.gsc = nxt("gsc", [128, 128], F32, 512)
        self.low_sb = nxt("low_sb", [128, 512], BF16, 1024)
        self.smat_sb = nxt("smat_sb", [128, 512], BF16, 1024)
        self.wft_sb = nxt("wft_sb", [8, 512], BF16, 1024)
        self.wfbf_sb = nxt("wfbf_sb", [128, 8], BF16, 32)
        self.junk = nxt("junk", [128, 8], F32, 32)
        assert FO <= 16512 + 212448, FO

    def _alloc_psum(self):
        nc = self.nc
        self.ps4 = nc.alloc_psum_tensor("ps4", [128, 2048], F32)
        self.psA = nc.alloc_psum_tensor("psA", [128, 512], F32)
        self.psB = nc.alloc_psum_tensor("psB", [128, 512], F32)
        self.psC = nc.alloc_psum_tensor("psC", [128, 512], F32)
        self.psD = nc.alloc_psum_tensor("psD", [128, 512], F32)

    def _build_phases(self):
        self.ph0_consts()
        self.ph12_qkv()
        if "qkv" in self.dumps:
            self.dump_qkv()
        if self._on("p3"):
            self.ph3_attention()
            if "o" in self.dumps:
                self.dump_o()
        if self._on("p4"):
            self.ph4_exchange()
        if self._on("p5"):
            self.ph5_outproj()
            if "u" in self.dumps:
                self.dump_u()
        if self._on("p6"):
            self.ph6_ffn()
            if "ffn" in self.dumps:
                self.dump_ffn()
        self.ph_final()

    def ph0_consts(self):
        @self.emit("sync")
        def _(e, en):
            v = 0
            for dst, src in [(self.bqk_sb, self.hbqk), (self.bv_sb, self.hbv),
                             (self.bo_sb, self.hbo), (self.bb_sb, self.hbb),
                             (self.ident_sb, self.hident),
                             (self.rexp_sb, self.hrexp),
                             (self.pidx_sb, self.hpidx),
                             (self.selA_sb, self.hselA),
                             (self.selB_sb, self.hselB)]:
                v = self.sXb.inc(e.dma_start(out=dst[:, :], in_=src[:, :]),
                                 16)
            self.mark("consts", self.sXb, v)

        @self.emit("gpsimd")
        def _(e, en):
            self.dmae(e, en, "sW4",
                      self.gw_sb[:, :].rearrange("p (kc x) -> p kc x", kc=KC),
                      self.hgw[:, :, :], mark="gw")

        @self.emit("vector")
        def _(e, en):
            e.memset(self.junk[:, :], 0.0)

        @self.emit("scalar")
        def _(e, en):
            e.memzero(self.junk[:, 4:8])

    def xln_ap(self, q, kc, c0, cn):
        if q % 2 == 0:
            return self.xln0[:, kc * 512 + c0: kc * 512 + c0 + cn]
        if kc < 24:
            return self.xln1a[:, kc * 512 + c0: kc * 512 + c0 + cn]
        return self.xln1b[:, (kc - 24) * 512 + c0: (kc - 24) * 512 + c0 + cn]

    def ph12_qkv(self):
        for q in range(NQT):
            for tt in range(4):
                self._ln1_tile(q, tt)
            self._msweep(q)
            self._vprod(q)

    def _ln1_tile(self, q, tt):
        ti = q * 4 + tt
        tok0 = q * 512 + tt * 128

        @self.emit("sync")
        def _(e, en, ti=ti, tok0=tok0):
            if ti > 0:
                self.wait(e, en, f"ln1_apply_{ti-1}")
            self.dmae(e, en, "sXa", self.xt[:, :],
                      self.hx[tok0:tok0 + 128, :], mark=f"xload_{ti}")

        @self.emit("scalar")
        def _(e, en, ti=ti, q=q, tt=tt):
            self.wait(e, en, f"xload_{ti}")
            a = self.ast
            v = None
            sA = self.sems["sACT"]
            for i in range(4):
                sl = self.xt[:, i * 1024:(i + 1) * 1024]
                if v is not None:
                    self.wait_val(e, en, sA, v)
                e.activation(self.ps4[:, 0:1024], sl, AF.Square,
                             accum_out=a[:, 8 + i:9 + i])
                v = self.sACT.inc(
                    e.activation(self.ps4[:, 1024:2048], sl, AF.Copy,
                                 accum_out=a[:, 12 + i:13 + i]))
            self.mark(f"ln1_sums_{ti}", self.sACT, v)

        @self.emit("vector")
        def _(e, en, ti=ti):
            self.wait(e, en, f"ln1_sums_{ti}")
            a = self.ast
            e.tensor_tensor(a[:, 2:3], a[:, 8:9], a[:, 9:10], OP.add)
            e.tensor_tensor(a[:, 3:4], a[:, 12:13], a[:, 13:14], OP.add)
            e.tensor_tensor(a[:, 16:17], a[:, 10:11], a[:, 11:12], OP.add)
            e.tensor_tensor(a[:, 17:18], a[:, 14:15], a[:, 15:16], OP.add)
            e.tensor_tensor(a[:, 2:3], a[:, 2:3], a[:, 16:17], OP.add)
            e.tensor_tensor(a[:, 3:4], a[:, 3:4], a[:, 17:18], OP.add)
            e.tensor_scalar(a[:, 4:5], a[:, 3:4], 1.0 / D, None, OP.mult)
            e.tensor_scalar(a[:, 5:6], a[:, 2:3], 1.0 / D, None, OP.mult)
            e.tensor_tensor(a[:, 6:7], a[:, 4:5], a[:, 4:5], OP.mult)
            e.tensor_tensor(a[:, 5:6], a[:, 5:6], a[:, 6:7], OP.subtract)
            v = self.sDVE.inc(
                e.tensor_scalar(a[:, 5:6], a[:, 5:6], 1.0, 1e-5,
                                OP.mult, OP.add))
            self.mark(f"ln1_var_{ti}", self.sDVE, v)

        @self.emit("scalar")
        def _(e, en, ti=ti):
            self.wait(e, en, f"ln1_var_{ti}")
            v = self.sACT.inc(
                e.activation(self.ast[:, 7:8], self.ast[:, 5:6], AF.Sqrt))
            self.mark(f"ln1_sd_{ti}", self.sACT, v)

        @self.emit("vector")
        def _(e, en, ti=ti):
            self.wait(e, en, f"ln1_sd_{ti}")
            a = self.ast
            e.reciprocal(a[:, 0:1], a[:, 7:8])
            e.tensor_copy(self.junk[:, 0:1], a[:, 4:5])
            e.tensor_tensor(a[:, 1:2], a[:, 4:5], a[:, 0:1], OP.mult)
            v = self.sDVE.inc(
                e.tensor_scalar(a[:, 1:2], a[:, 1:2], -1.0, None, OP.mult))
            self.mark(f"ln1_rb_{ti}", self.sDVE, v)

        @self.emit("scalar")
        def _(e, en, ti=ti, q=q, tt=tt):
            a = self.ast
            self.wait(e, en, f"ln1_rb_{ti}")
            v = self.sACT.inc(
                e.activation(self.xln_tm[:, :], self.xt[:, :], AF.Identity,
                             scale=a[:, 0:1], bias=a[:, 1:2]))
            self.mark(f"ln1_apply_{ti}", self.sACT, v)

        @self.emit("sync")
        def _(e, en, ti=ti, q=q, tt=tt):
            self.wait(e, en, f"ln1_apply_{ti}")
            if q >= 2:
                self.wait(e, en, f"vmm_last_{q-2}")
            lane = "sT0" if q % 2 == 0 else "sT1"

            if q % 2 == 0:
                dst = self.xln0[:, :].rearrange("p (kc t) -> p kc t", kc=KC)
                self.dmae(e, en, lane, dst[:, :, tt * 128:(tt + 1) * 128],
                          self.xln_tm[:, :], mark=f"xln_tr_{ti}",
                          transpose=True, selfgate=False)
            else:
                d1 = self.xln1a[:, :].rearrange("p (kc t) -> p kc t", kc=24)
                d2 = self.xln1b[:, :].rearrange("p (kc t) -> p kc t", kc=8)
                self.dmae(e, en, lane, d1[:, :, tt * 128:(tt + 1) * 128],
                          self.xln_tm[:, 0:24 * 128], transpose=True,
                          selfgate=False)
                self.dmae(e, en, lane, d2[:, :, tt * 128:(tt + 1) * 128],
                          self.xln_tm[:, 24 * 128:], mark=f"xln_tr_{ti}",
                          transpose=True, selfgate=False)


    def _msweep(self, q):
        for mc in range(NMC):
            gi = q * NMC + mc
            sli = gi % 2

            @self.emit("gpsimd")
            def _(e, en, q=q, mc=mc, gi=gi, sli=sli):
                if gi >= 2:
                    self.wait(e, en, f"qk_mm_{gi-2}")
                self.dmae(e, en, f"sW{gi%2}",
                          self.wqksl[sli][:, :].rearrange(
                              "p (kc m) -> p kc m", kc=KC),
                          self.hwqk[mc, :, :, :], mark=f"wqk_{gi}")

            @self.emit("tensor")
            def _(e, en, q=q, mc=mc, gi=gi, sli=sli):
                self.wait(e, en, f"wqk_{gi}")
                self.wait(e, en, f"xln_tr_{q*4+3}")
                if gi >= 2:
                    self.wait(e, en, f"qk_evac_{gi-2}")
                if mc < 2 and q > 0:
                    self.wait(e, en, f"v_evac_{q-1}_1")
                ps = self.psA if gi % 2 == 0 else self.psB
                ins = None
                for kc in range(KC):
                    ins = e.matmul(ps[:, :],
                                   self.wqksl[sli][:, kc * 128:(kc + 1) * 128],
                                   self.xln_ap(q, kc, 0, 512),
                                   start=(kc == 0), stop=(kc == KC - 1))
                v = self.sPE.inc(ins)
                self.mark(f"qk_mm_{gi}", self.sPE, v)

            @self.emit("scalar")
            def _(e, en, q=q, mc=mc, gi=gi):
                self.wait(e, en, f"qk_mm_{gi}")
                self.wait(e, en, "consts")
                ps = self.psA if gi % 2 == 0 else self.psB
                if mc < HQC:
                    dst = self.qo[mc][:, q * 512:(q + 1) * 512]
                else:
                    dst = self.Ksb[mc - HQC][:, q * 512:(q + 1) * 512]
                v = self.sACT.inc(e.activation(dst, ps[:, :], AF.Identity,
                                               bias=self.bqk_sb[:, mc:mc + 1]))
                self.mark(f"qk_evac_{gi}", self.sACT, v)

    def _vprod(self, q):
        for kc in range(KC):
            gi = q * KC + kc
            sli = gi % 2

            @self.emit("gpsimd")
            def _(e, en, gi=gi, kc=kc, sli=sli):
                if gi >= 2:
                    self.wait(e, en, f"v_mm_{gi-2}")
                self.dmae(e, en, f"sW{2+gi%2}", self.wvsl[sli][:, :],
                          self.hwv[:, kc, :], mark=f"wv_{gi}")

            @self.emit("tensor")
            def _(e, en, q=q, kc=kc, gi=gi, sli=sli):
                self.wait(e, en, f"wv_{gi}")
                if kc == 0:
                    self.wait(e, en, f"qk_evac_{q*NMC+NMC-1}")
                    if q > 0:
                        self.wait(e, en, f"v_evac_{q-1}_3")
                ins = None
                for tt, ps in enumerate([self.psA, self.psB,
                                         self.psC, self.psD]):
                    ins = e.matmul(ps[:, :],
                                   self.xln_ap(q, kc, tt * 128, 128),
                                   self.wvsl[sli][:, :],
                                   start=(kc == 0), stop=(kc == KC - 1))
                v = self.sPE.inc(ins)
                self.mark(f"v_mm_{gi}", self.sPE, v)
                if kc == KC - 1:
                    self.mark(f"vmm_last_{q}", self.sPE, v)

        for tt in range(4):
            @self.emit("scalar")
            def _(e, en, q=q, tt=tt):
                self.wait(e, en, f"v_mm_{q*KC+KC-1}")
                ps = [self.psA, self.psB, self.psC, self.psD][tt]
                v = self.sACT.inc(e.activation(
                    self.v_sb[:, (q * 4 + tt) * NVC:(q * 4 + tt + 1) * NVC],
                    ps[:, :], AF.Copy))
                self.mark(f"v_evac_{q}_{tt}", self.sACT, v)

    def dump_qkv(self):
        dq = self.dumper("q", [HQC, 128, S])
        dk = self.dumper("k", [HKVC, 128, S])
        dv = self.dumper("v", [S, NVC])

        @self.emit("gpsimd")
        def _(e, en):
            self.wait(e, en, f"qk_evac_{NQT*NMC-1}")
            for t in range(4):
                self.wait(e, en, f"v_evac_{NQT-1}_{t}")
            v = 0
            for i in range(HQC):
                v = self.sSt2.inc(e.dma_start(out=dq[i, :, :],
                                              in_=self.qo[i][:, :]), 16)
            for i in range(HKVC):
                v = self.sSt2.inc(e.dma_start(out=dk[i, :, :],
                                              in_=self.Ksb[i][:, :]), 16)
            v = self.sSt2.inc(e.dma_start(
                out=dv[:, :].rearrange("(a p) x -> p a x", p=128),
                in_=self.v_sb[:, :].rearrange("p (a x) -> p a x", a=16)), 16)
            self.mark("dump_qkv", self.sSt2, v)

    pass


def build_kernel(upto="p6", dumps=(), detect_races=False):
    mk = MK(upto=upto, dumps=dumps, detect_races=detect_races)
    nc = mk.finalize()
    return nc, mk





HD = 128
HQC, HKVC = 16, 4
KC = 32
NVC = 512
NMC = 20
NQT = 4
SCALE = 1.0 / math.sqrt(HD)
S = 2048
TOK = 1024
D = 4096
N_EXP = 8


def _pu_cols(pu):
    b = 8 + pu * 5
    return dict(nmraw=b, tmp=b + 1, farg=b + 2, f=b + 3, lsub=b + 4)


def _po_cols(po):
    b = 18 + po * 4
    return dict(mrun0=b, mrun1=b + 1, lrun=b + 2, rl=b + 3)


def _units():
    us = []
    for hh in range(HQC):
        for qt in range(16):
            klen = (qt + 1) * 128
            subs = [(0, min(klen, 1024))]
            if klen > 1024:
                subs.append((1024, klen - 1024))
            for si, (koff, kl) in enumerate(subs):
                us.append(dict(hh=hh, qt=qt, si=si, koff=koff, kl=kl,
                               last=(si == len(subs) - 1)))
    for i, u in enumerate(us):
        u["i"] = i
    return us


def ph3_attention(self):
    units = _units()
    a = self.ast

    @self.emit("sync")
    def _(e, en):
        self.wait(e, en, "vmm_last_3")
        self.sXb.inc(e.dma_start(out=self.rc_sb[:, :], in_=self.hrc[:, :]),
                     16)
        self.sXb.inc(e.dma_start(out=self.cm_sb[:, :], in_=self.hcmask[:, :]),
                     16)
        self.sXb.inc(e.dma_start(out=self.perm_sb[:, :],
                                 in_=self.hperm[:, :]), 16)
        v = self.sXb.inc(e.dma_start(out=self.rs_sb[:, :],
                                     in_=self.hrs[:, :]), 16)
        self.mark("ropetabs", self.sXb, v)

    self._rope_n = 0

    def rope_buf(name, buf, evac_ev):
        for cb in range(4):
            ri = self._rope_n
            self._rope_n += 1
            ps = self.psC if ri % 2 == 0 else self.psD

            @self.emit("tensor")
            def _(e, en, ri=ri, cb=cb, buf=buf, ps=ps, evac_ev=evac_ev):
                self.wait(e, en, evac_ev)
                self.wait(e, en, "ropetabs")
                if ri < 2:
                    self.wait(e, en, "v_evac_3_2")
                    self.wait(e, en, "v_evac_3_3")
                if ri >= 2:
                    self.wait(e, en, f"rope_dve_{ri-2}")
                v = self.sPE.inc(e.matmul(ps[:, :], self.perm_sb[:, :],
                                          buf[:, cb * 512:(cb + 1) * 512],
                                          start=True, stop=True))
                self.mark(f"rope_mm_{ri}", self.sPE, v)

            @self.emit("vector")
            def _(e, en, ri=ri, cb=cb, buf=buf, ps=ps, name=name):
                self.wait(e, en, f"rope_mm_{ri}")
                self.wait(e, en, "ropetabs")
                t1 = self.t1_scr[ri % 2]
                c0, c1 = cb * 512, (cb + 1) * 512
                e.tensor_tensor(t1[:, :], buf[:, c0:c1], self.rc_sb[:, c0:c1],
                                OP.mult)
                e.tensor_tensor(buf[:, c0:c1], ps[:, :], self.rs_sb[:, c0:c1],
                                OP.mult)
                v = self.sDVE.inc(
                    e.tensor_tensor(buf[:, c0:c1], buf[:, c0:c1], t1[:, :],
                                    OP.add))
                self.mark(f"rope_dve_{ri}", self.sDVE, v)
                if cb == 3:
                    self.mark(f"rope_done_{name}", self.sDVE, v)

    for i in range(HKVC):
        rope_buf(f"k{i}", self.Ksb[i], f"qk_evac_{3*NMC + HQC + i}")

    def emit_S(u):
        i, hh, qt = u["i"], u["hh"], u["qt"]
        pu = i % 2

        @self.emit("tensor")
        def _(e, en, u=u, i=i, hh=hh, qt=qt, pu=pu):
            self.wait(e, en, f"rope_done_q{hh}")
            self.wait(e, en, f"rope_done_k{hh//4}")
            if i >= 2:
                self.wait(e, en, f"exp_{i-2}")
            ins = None
            for kb in range(0, u["kl"], 512):
                nb = min(512, u["kl"] - kb)
                ins = e.matmul(
                    self.ps4[:, pu * 1024 + kb: pu * 1024 + kb + nb],
                    self.qo[hh][:, qt * 128:(qt + 1) * 128],
                    self.Ksb[hh // 4][:, u["koff"] + kb: u["koff"] + kb + nb],
                    start=True, stop=True)
            v = self.sPE.inc(ins)
            self.mark(f"smm_{i}", self.sPE, v)

    def emit_softmax(u):
        i, qt = u["i"], u["qt"]
        pu = i % 2
        po = (u["hh"] * 16 + qt) % 2
        pc, ocl = _pu_cols(pu), _po_cols(po)

        @self.emit("vector")
        def _(e, en, u=u, i=i, pu=pu, pc=pc, ocl=ocl):
            self.wait(e, en, f"smm_{i}")
            self.wait(e, en, "ropetabs")
            base = pu * 1024
            if u["koff"] + u["kl"] == (u["qt"] + 1) * 128:
                off = base + u["kl"] - 128
                e.tensor_tensor(self.ps4[:, off:off + 128],
                                self.ps4[:, off:off + 128],
                                self.cm_sb[:, :], OP.add)
            e.tensor_reduce(a[:, pc["nmraw"]:pc["nmraw"] + 1],
                            self.ps4[:, base:base + u["kl"]],
                            AXX, OP.max, negate=True)
            cur = ocl["mrun0"] if u["si"] % 2 == 0 else ocl["mrun1"]
            prv = ocl["mrun1"] if u["si"] % 2 == 0 else ocl["mrun0"]
            if u["si"] == 0:
                v = self.sDVE.inc(
                    e.tensor_scalar(a[:, cur:cur + 1],
                                    a[:, pc["nmraw"]:pc["nmraw"] + 1],
                                    SCALE, None, OP.mult))
            else:
                e.tensor_scalar(a[:, cur:cur + 1],
                                a[:, pc["nmraw"]:pc["nmraw"] + 1],
                                SCALE, a[:, prv:prv + 1], OP.mult, OP.min)
                v = self.sDVE.inc(
                    e.tensor_tensor(a[:, pc["farg"]:pc["farg"] + 1],
                                    a[:, cur:cur + 1], a[:, prv:prv + 1],
                                    OP.subtract))
            self.mark(f"sfx_{i}", self.sDVE, v)

        @self.emit("scalar")
        def _(e, en, u=u, i=i, pu=pu, pc=pc, ocl=ocl):
            self.wait(e, en, f"sfx_{i}")
            if i >= 2:
                self.wait(e, en, f"ptr_{i-2}")
            if u["si"] == 1:
                v = self.sACT.inc(
                    e.activation(a[:, pc["f"]:pc["f"] + 1],
                                 a[:, pc["farg"]:pc["farg"] + 1], AF.Exp))
                self.mark(f"fexp_{i}", self.sACT, v)
            base = pu * 1024
            cur = ocl["mrun0"] if u["si"] % 2 == 0 else ocl["mrun1"]
            v = self.sACT.inc(
                e.activation(self.p_scr[pu][:, 0:u["kl"]],
                             self.ps4[:, base:base + u["kl"]], AF.Exp,
                             scale=SCALE,
                             bias=a[:, cur:cur + 1],
                             accum_out=a[:, pc["lsub"]:pc["lsub"] + 1]))
            self.mark(f"exp_{i}", self.sACT, v)

        @self.emit("sync")
        def _(e, en, u=u, i=i, pu=pu):
            if i >= 2:
                self.wait(e, en, f"omm_{i-2}")
            self.wait(e, en, f"exp_{i}")
            nt = u["kl"] // 128
            dst = self.pt_scr[pu][:, 0:u["kl"]].rearrange(
                "p (t q) -> p t q", t=nt)
            self.dmae(e, en, f"sTp{i%2}", dst,
                      self.p_scr[pu][:, 0:u["kl"]], mark=f"ptr_{i}",
                      transpose=True)

    def emit_lupd(u):
        i, qt = u["i"], u["qt"]
        pu = i % 2
        po = (u["hh"] * 16 + qt) % 2
        pc, ocl = _pu_cols(pu), _po_cols(po)

        @self.emit("vector")
        def _(e, en, u=u, i=i, pu=pu, pc=pc, ocl=ocl, po=po):
            if u["si"] == 0:
                self.wait(e, en, f"exp_{i}")
                v = self.sDVE.inc(
                    e.tensor_copy(a[:, ocl["lrun"]:ocl["lrun"] + 1],
                                  a[:, pc["lsub"]:pc["lsub"] + 1]))
                self.mark(f"lupd_{i}", self.sDVE, v)
            else:
                self.wait(e, en, f"fexp_{i}")
                self.wait(e, en, f"exp_{i}")
                self.wait(e, en, f"omm_{i-1}")
                ps = self.psA if po == 0 else self.psB
                e.tensor_scalar(ps[:, 0:128], ps[:, 0:128],
                                a[:, pc["f"]:pc["f"] + 1], None, OP.mult)
                v = self.sDVE.inc(
                    e.scalar_tensor_tensor(
                        a[:, ocl["lrun"]:ocl["lrun"] + 1],
                        a[:, ocl["lrun"]:ocl["lrun"] + 1],
                        a[:, pc["f"]:pc["f"] + 1],
                        a[:, pc["lsub"]:pc["lsub"] + 1],
                        OP.mult, OP.add))
                self.mark(f"lupd_{i}", self.sDVE, v)

    def emit_O(u):
        i, hh, qt = u["i"], u["hh"], u["qt"]
        pu = i % 2
        po = (hh * 16 + qt) % 2
        oi = hh * 16 + qt
        kv = hh // 4
        ocl = _po_cols(po)

        @self.emit("tensor")
        def _(e, en, u=u, i=i, pu=pu, po=po, oi=oi, kv=kv):
            self.wait(e, en, f"ptr_{i}")
            if u["si"] == 1:
                self.wait(e, en, f"lupd_{i}")
            if u["si"] == 0 and oi >= 2:
                self.wait(e, en, f"oevac_{oi-2}")
            ps = self.psA if po == 0 else self.psB
            ins = None
            nt = u["kl"] // 128
            for kt in range(nt):
                tok = u["koff"] + kt * 128
                ti = tok // 128
                vcol = ti * NVC + kv * 128
                ins = e.matmul(
                    ps[:, 0:128],
                    self.pt_scr[pu][:, kt * 128:(kt + 1) * 128],
                    self.v_sb[:, vcol:vcol + 128],
                    start=(u["si"] == 0 and kt == 0),
                    stop=(kt == nt - 1),
                    skip_group_check=(u["si"] == 1))
            v = self.sPE.inc(ins)
            self.mark(f"omm_{i}", self.sPE, v)

        if not u["last"]:
            return

        @self.emit("vector")
        def _(e, en, i=i, ocl=ocl, oi=oi):
            self.wait(e, en, f"omm_{i}")
            e.tensor_copy(self.junk[:, 0:1], a[:, 0:1])
            v = self.sDVE.inc(
                e.reciprocal(a[:, ocl["rl"]:ocl["rl"] + 1],
                             a[:, ocl["lrun"]:ocl["lrun"] + 1]))
            self.mark(f"rl_{oi}", self.sDVE, v)

        @self.emit("scalar")
        def _(e, en, i=i, po=po, oi=oi, ocl=ocl, hh=hh, qt=qt):
            self.wait(e, en, f"omm_{i}")
            self.wait(e, en, f"rl_{oi}")
            if oi >= 2:
                self.wait(e, en, f"oevac_{oi-2}")
            ps = self.psA if po == 0 else self.psB
            v = self.sACT.inc(
                e.activation(self.otmp[po][:, :], ps[:, 0:128], AF.Identity,
                             scale=a[:, ocl["rl"]:ocl["rl"] + 1]))
            self.mark(f"oevac_a_{oi}", self.sACT, v)

        @self.emit("sync")
        def _(e, en, po=po, oi=oi, hh=hh, qt=qt):
            self.wait(e, en, f"oevac_a_{oi}")
            dst = self.qo[hh][:, qt * 128:(qt + 1) * 128].rearrange(
                "p (t q) -> p t q", t=1)
            self.dmae(e, en, f"sTo{oi%2}", dst, self.otmp[po][:, :],
                      mark=f"oevac_{oi}", transpose=True)

    prev = None
    for u in units:
        if u["qt"] == 0 and u["si"] == 0:
            rope_buf(f"q{u['hh']}", self.qo[u["hh"]],
                     f"qk_evac_{3*NMC + u['hh']}")
        emit_S(u)
        emit_softmax(u)
        emit_lupd(u)
        if prev is not None:
            emit_O(prev)
            if prev["last"] and prev["qt"] == 15 and self._on("p4"):
                self.ph4_head(prev["hh"])
        prev = u
    emit_O(prev)
    if self._on("p4"):
        self.ph4_head(prev["hh"])


def dump_o(self):
    do = self.dumper("o", [HQC, 128, S])

    @self.emit("gpsimd")
    def _(e, en):
        v = 0
        for hh in range(HQC):
            self.wait(e, en, f"oevac_{hh*16+15}")
            v = self.sSt2.inc(e.dma_start(out=do[hh, :, :],
                                          in_=self.qo[hh][:, :]), 16)
        self.mark("dump_o", self.sSt2, v)


def ph4_head(self, hh):
    @self.emit("vector")
    def _(e, en, hh=hh):
        self.wait(e, en, f"oevac_{hh*16+15}")
        q = self.qo[hh]
        osl = self.o_sel[hh]
        e.tensor_scalar(osl[:, :], q[:, 0:1024], self.selB_sb[:, 0:1],
                        self.bv_sb[:, hh:hh + 1], OP.mult, OP.add)
        v = self.sDVE.inc(
            e.scalar_tensor_tensor(osl[:, :], q[:, 1024:2048],
                                   self.selA_sb[:, 0:1], osl[:, :],
                                   OP.mult, OP.add))
        self.mark(f"slabsel_{hh}", self.sDVE, v)
        self.wait(e, en, f"slabdma_{hh}")
        e.tensor_scalar(osl[:, :], q[:, 0:1024], self.selA_sb[:, 0:1],
                        self.bv_sb[:, hh:hh + 1], OP.mult, OP.add)
        v = self.sDVE.inc(
            e.scalar_tensor_tensor(osl[:, :], q[:, 1024:2048],
                                   self.selB_sb[:, 0:1], osl[:, :],
                                   OP.mult, OP.add))
        self.mark(f"ownsel_{hh}", self.sDVE, v)

    @self.emit("gpsimd")
    def _(e, en, hh=hh):
        self.wait(e, en, f"slabsel_{hh}")
        self.dmae(e, en, f"sSt{hh%2}",
                  self.cc_in[hh * 128:(hh + 1) * 128, :],
                  self.o_sel[hh][:, :], mark=f"slabdma_{hh}")


def ph4_exchange(self):
    @self.emit("gpsimd")
    def _(e, en):
        for hh in range(HQC):
            self.wait(e, en, f"slabdma_{hh}")
        v = self.sCC.inc(e.collective_compute(
            "AllGather", OP.bypass,
            replica_groups=[list(range(8))],
            ins=[self.cc_in[:, :]],
            outs=[self.cc_out[:, :]]))
        self.mark("cc", self.sCC, v)
        self.wait(e, en, "cc")
        for j in range(16):
            v = self.sCC.inc(e.indirect_dma_start(
                out=self.o_oth[j][:, :],
                out_offset=None,
                in_=self.cc_out[0:128, :],
                in_offset=IndirectOffsetOnAxis(self.pidx_sb[:, j:j + 1],
                                               axis=0)), 16)
            self.mark(f"ooth_{j}", self.sCC, v)


def ph5_outproj(self):
    for oc in range(32):
        sli = oc % 2

        @self.emit("gpsimd")
        def _(e, en, oc=oc, sli=sli):
            if oc < 2:
                self.wait(e, en, f"ownsel_{oc}")
            if oc >= 2:
                self.wait(e, en, f"opmm_{oc-2}_1")
            self.dmae(e, en, f"sW{oc%2}",
                      self.wosl[sli][:, :].rearrange("p (kc m) -> p kc m",
                                                     kc=KC),
                      self.hwo[oc, :, :, :], mark=f"wo_{oc}")

        @self.emit("tensor")
        def _(e, en, oc=oc, sli=sli):
            self.wait(e, en, f"wo_{oc}")
            if oc == 0:
                self.wait(e, en, f"ownsel_{HQC-1}")
                self.wait(e, en, f"ooth_{15}")
            pa, pb = ((self.psA, self.psB) if oc % 2 == 0
                      else (self.psC, self.psD))
            if oc >= 2:
                self.wait(e, en, f"uevac_{oc-2}_1")
            ins = None
            for dc in range(32):
                rhs = (self.o_sel[dc] if dc < 16 else self.o_oth[dc - 16])
                e.matmul(pa[:, :],
                         self.wosl[sli][:, dc * 128:(dc + 1) * 128],
                         rhs[:, 0:512], start=(dc == 0), stop=(dc == 31))
                ins = e.matmul(pb[:, :],
                               self.wosl[sli][:, dc * 128:(dc + 1) * 128],
                               rhs[:, 512:1024],
                               start=(dc == 0), stop=(dc == 31))
            v = self.sPE.inc(ins)
            self.mark(f"opmm_{oc}_1", self.sPE, v)

        for g in range(2):
            @self.emit("scalar")
            def _(e, en, oc=oc, g=g):
                self.wait(e, en, f"opmm_{oc}_1")
                pa, pb = ((self.psA, self.psB) if oc % 2 == 0
                          else (self.psC, self.psD))
                ps = pa if g == 0 else pb
                uv = self.u_all[:, :].rearrange(
                    "p (ti oc t) -> p ti oc t", ti=8, oc=32)
                v = self.sACT.inc(e.activation(
                    uv[:, g * 4:(g + 1) * 4, oc, :],
                    ps[:, :].rearrange("p (ti t) -> p ti t", ti=4),
                    AF.Identity, bias=self.bo_sb[:, oc:oc + 1]))
                self.mark(f"uevac_{oc}_{g}", self.sACT, v)


def dump_u(self):
    du = self.dumper("u", [128, 8 * 32 * 128], F32)

    @self.emit("gpsimd")
    def _(e, en):
        for oc in range(32):
            self.wait(e, en, f"uevac_{oc}_1")
        v = self.sSt2.inc(e.dma_start(out=du[:, :], in_=self.u_all[:, :]),
                          16)
        self.mark("dump_u", self.sSt2, v)


def ph6_ffn(self):
    for ti in range(8):
        @self.emit("sync")
        def _(e, en, ti=ti):
            for oc in range(32):
                self.wait(e, en, f"uevac_{oc}_{ti//4}")
            dst = self.u_tm[ti][:, :].rearrange("p (oc t) -> p oc t", oc=32)
            self.dmae(e, en, f"sTp{ti%2}", dst,
                      self.u_all[:, ti * D:(ti + 1) * D], mark=f"utr_{ti}",
                      transpose=True)

    @self.emit("gpsimd")
    def _(e, en):
        self.wait(e, en, "utr_7")
        self.dmae(e, en, "sW4",
                  self.la_sb[:, :].rearrange("p (kc r) -> p kc r", kc=KC),
                  self.hla[:, :, :], mark="la")
        self.dmae(e, en, "sW4", self.lb_sb[:, :], self.hlb[:, :], mark="lb")

    for g in range(2):
        self._ph6_stage1b(g)
        self._ph6_stage2(g)
        self._ph6_stage3(g)


def _ln_stats(self, e, src, ti, pfx):
    ln = self.lnst
    v = None
    sA = self.sems["sACT"]
    for i in range(4):
        sl = src[:, i * 1024:(i + 1) * 1024]
        if v is not None:
            self.wait_val(e, "scalar", sA, v)
        e.activation(self.ps4[:, 0:1024], sl, AF.Square,
                     accum_out=ln[:, 40 + i:41 + i])
        v = self.sACT.inc(
            e.activation(self.ps4[:, 1024:2048], sl, AF.Copy,
                         accum_out=ln[:, 44 + i:45 + i]))
    self.mark(f"{pfx}_sums_{ti}", self.sACT, v)


def _ln_finish(self, e, pfx, ti):
    ln = self.lnst
    e.tensor_tensor(ln[:, 48:49], ln[:, 40:41], ln[:, 41:42], OP.add)
    e.tensor_tensor(ln[:, 49:50], ln[:, 42:43], ln[:, 43:44], OP.add)
    e.tensor_tensor(ln[:, 50:51], ln[:, 44:45], ln[:, 45:46], OP.add)
    e.tensor_tensor(ln[:, 51:52], ln[:, 46:47], ln[:, 47:48], OP.add)
    e.tensor_tensor(ln[:, 48:49], ln[:, 48:49], ln[:, 49:50], OP.add)
    e.tensor_tensor(ln[:, 50:51], ln[:, 50:51], ln[:, 51:52], OP.add)
    e.tensor_scalar(ln[:, 52:53], ln[:, 50:51], 1.0 / D, None, OP.mult)
    e.tensor_scalar(ln[:, 53:54], ln[:, 48:49], 1.0 / D, None, OP.mult)
    e.tensor_tensor(ln[:, 54:55], ln[:, 52:53], ln[:, 52:53], OP.mult)
    e.tensor_tensor(ln[:, 53:54], ln[:, 53:54], ln[:, 54:55], OP.subtract)
    v = self.sDVE.inc(
        e.tensor_scalar(ln[:, 53:54], ln[:, 53:54], 1.0, 1e-5,
                        OP.mult, OP.add))
    self.mark(f"{pfx}_var_{ti}", self.sDVE, v)


def _ln_rb(self, e, en, rcol, cbcol, pfx, ti):
    ln = self.lnst
    self.wait(e, en, f"{pfx}_sd_{ti}")
    e.reciprocal(ln[:, rcol:rcol + 1], ln[:, 55:56])
    e.tensor_copy(self.junk[:, 0:1], ln[:, 52:53])
    e.tensor_tensor(ln[:, cbcol:cbcol + 1], ln[:, 52:53],
                    ln[:, rcol:rcol + 1], OP.mult)
    v = self.sDVE.inc(
        e.tensor_scalar(ln[:, cbcol:cbcol + 1], ln[:, cbcol:cbcol + 1],
                        -1.0, None, OP.mult))
    self.mark(f"{pfx}_rb_{ti}", self.sDVE, v)


def _ph6_stage1b(self, g):
    for tt in range(4):
        ti = g * 4 + tt
        r2c, cb2c = ti * 4, ti * 4 + 1

        @self.emit("sync")
        def _(e, en, ti=ti, g=g, tt=tt):
            if ti == 0:
                self.wait(e, en, "uevac_31_1")
            if ti > 0:
                self.wait(e, en, f"x1add_{ti-1}")
            if g == 1 and tt == 0:
                self.wait(e, en, f"x2_{3}")
                self.wait(e, en, f"store_{3}")
            tok0 = g * 512 + tt * 128
            self.dmae(e, en, "sXa", self.xobuf[:, :],
                      self.hxo[tok0:tok0 + 128, :], mark=f"xoload_{ti}")

        @self.emit("scalar")
        def _(e, en, ti=ti, g=g, tt=tt, r2c=r2c, cb2c=cb2c):
            self.wait(e, en, f"utr_{ti}")
            if ti > 0:
                self.wait(e, en, f"ln3ap_{ti-1}")
            if g == 1 and tt == 0:
                self.wait(e, en, f"x2_{3}")
            _ln_stats(self, e, self.u_tm[ti], ti, "ln2")

        @self.emit("vector")
        def _(e, en, ti=ti):
            self.wait(e, en, f"ln2_sums_{ti}")
            _ln_finish(self, e, "ln2", ti)

        @self.emit("scalar")
        def _(e, en, ti=ti):
            self.wait(e, en, f"ln2_var_{ti}")
            v = self.sACT.inc(
                e.activation(self.lnst[:, 55:56], self.lnst[:, 53:54],
                             AF.Sqrt))
            self.mark(f"ln2_sd_{ti}", self.sACT, v)

        @self.emit("scalar")
        def _(e, en, ti=ti, r2c=r2c, cb2c=cb2c):
            self.wait(e, en, f"ln2_rb_{ti}")
            v = self.sACT.inc(
                e.activation(self.x1buf[:, :], self.u_tm[ti][:, :],
                             AF.Identity,
                             scale=self.lnst[:, r2c:r2c + 1],
                             bias=self.lnst[:, cb2c:cb2c + 1]))
            self.mark(f"ln2ap_{ti}", self.sACT, v)

        @self.emit("vector")
        def _(e, en, ti=ti, r2c=r2c, cb2c=cb2c):
            _ln_rb(self, e, en, r2c, cb2c, "ln2", ti)
            self.wait(e, en, f"ln2ap_{ti}")
            self.wait(e, en, f"xoload_{ti}")
            v = self.sDVE.inc(
                e.tensor_tensor(self.x1buf[:, :], self.x1buf[:, :],
                                self.xobuf[:, :], OP.add))
            self.mark(f"x1add_{ti}", self.sDVE, v)

        @self.emit("scalar")
        def _(e, en, ti=ti):
            self.wait(e, en, f"x1add_{ti}")
            _ln_stats(self, e, self.x1buf, ti, "ln3")

        @self.emit("vector")
        def _(e, en, ti=ti):
            self.wait(e, en, f"ln3_sums_{ti}")
            _ln_finish(self, e, "ln3", ti)

        @self.emit("scalar")
        def _(e, en, ti=ti):
            self.wait(e, en, f"ln3_var_{ti}")
            v = self.sACT.inc(
                e.activation(self.lnst[:, 55:56], self.lnst[:, 53:54],
                             AF.Sqrt))
            self.mark(f"ln3_sd_{ti}", self.sACT, v)

        @self.emit("vector")
        def _(e, en, ti=ti):
            _ln_rb(self, e, en, 56, 57, "ln3", ti)

        @self.emit("scalar")
        def _(e, en, ti=ti, g=g, tt=tt):
            self.wait(e, en, f"ln3_rb_{ti}")
            v = self.sACT.inc(
                e.activation(self.xstage[:, :], self.x1buf[:, :], AF.Identity,
                             scale=self.lnst[:, 56:57],
                             bias=self.lnst[:, 57:58]))
            self.mark(f"ln3ap_{ti}", self.sACT, v)

        @self.emit("sync")
        def _(e, en, ti=ti, g=g, tt=tt):
            self.wait(e, en, f"ln3ap_{ti}")
            self.wait(e, en, f"utr_{g*4+3}")
            if g == 1 and tt == 0:
                self.wait(e, en, f"ffnmm_last_{g-1}")
            dst = self.xln3_sb[:, :].rearrange("p (kc t) -> p kc t", kc=KC)
            self.dmae(e, en, f"sT{g%2}",
                      dst[:, :, tt * 128:(tt + 1) * 128],
                      self.xstage[:, :], mark=f"xln3tr_{ti}", transpose=True,
                      selfgate=False)


def _ph6_stage2(self, g):
    def xl(kc, c0, cn):
        return self.xln3_sb[:, kc * 512 + c0:kc * 512 + c0 + cn]

    @self.emit("tensor")
    def _(e, en, g=g):
        self.wait(e, en, f"xln3tr_{g*4+3}")
        self.wait(e, en, "gw")
        if g == 0:
            self.wait(e, en, "uevac_31_1")
        else:
            self.wait(e, en, f"ffnevac_{g-1}_31")
            self.wait(e, en, f"smatevac_{g-1}")
            self.wait(e, en, f"lowc_{g-1}")
        ins = None
        for kc in range(KC):
            for tt in range(4):
                ins = e.matmul(
                    self.psC[:, tt * 8:(tt + 1) * 8],
                    xl(kc, tt * 128, 128),
                    self.gw_sb[:, kc * N_EXP:(kc + 1) * N_EXP],
                    start=(kc == 0 and tt == 0), stop=(kc == KC - 1),
                    skip_group_check=True)
        v = self.sPE.inc(ins)
        self.mark(f"gatemm_{g}", self.sPE, v)

    @self.emit("scalar")
    def _(e, en, g=g):
        self.wait(e, en, f"gatemm_{g}")
        v = self.sACT.inc(e.activation(self.gsc[:, 0:32], self.psC[:, 0:32],
                                       AF.Copy))
        self.mark(f"gevac_{g}", self.sACT, v)

    for tt in range(4):
        gb = 32 + tt * 8
        wfb = 64 + tt * 8
        inb = 96 + tt * 8

        @self.emit("vector")
        def _(e, en, g=g, tt=tt, gb=gb, wfb=wfb, inb=inb):
            gs = self.gsc
            gv = gs[:, tt * 8:(tt + 1) * 8]
            self.wait(e, en, f"gevac_{g}")
            e.tensor_reduce(gs[:, gb:gb + 1], gv, AXX, OP.max, negate=True)
            e.tensor_copy(self.junk[:, 0:1], gs[:, gb:gb + 1])
            e.tensor_scalar(gs[:, inb:inb + 8], gv, gs[:, gb:gb + 1], None,
                            OP.add)
            e.tensor_single_scalar(gs[:, inb:inb + 8], gs[:, inb:inb + 8],
                                   0.0, OP.is_lt)
            e.tensor_scalar(gs[:, inb:inb + 8], gs[:, inb:inb + 8],
                            1.0e30, -1.0e30, OP.mult, OP.add)
            e.tensor_tensor(gs[:, inb:inb + 8], gv, gs[:, inb:inb + 8],
                            OP.add)
            e.tensor_reduce(gs[:, gb + 1:gb + 2], gs[:, inb:inb + 8], AXX,
                            OP.max, negate=True)
            e.tensor_copy(self.junk[:, 0:1], gs[:, gb:gb + 1])
            e.tensor_tensor(gs[:, gb + 2:gb + 3], gs[:, gb:gb + 1],
                            gs[:, gb + 1:gb + 2], OP.subtract)
            v = self.sDVE.inc(
                e.tensor_scalar(gs[:, inb:inb + 8], gv,
                                gs[:, gb + 1:gb + 2], 0.0, OP.add, OP.is_ge))
            self.mark(f"top2a_{g}_{tt}", self.sDVE, v)

        @self.emit("scalar")
        def _(e, en, g=g, tt=tt, gb=gb, wfb=wfb):
            gs = self.gsc
            self.wait(e, en, f"top2a_{g}_{tt}")
            e.activation(gs[:, gb + 3:gb + 4], gs[:, gb + 2:gb + 3], AF.Exp)
            v = self.sACT.inc(
                e.activation(gs[:, wfb:wfb + 8], gs[:, tt * 8:(tt + 1) * 8],
                             AF.Exp, bias=gs[:, gb:gb + 1]))
            self.mark(f"top2b_{g}_{tt}", self.sACT, v)

        @self.emit("vector")
        def _(e, en, g=g, tt=tt, gb=gb, wfb=wfb, inb=inb):
            gs = self.gsc
            self.wait(e, en, f"top2b_{g}_{tt}")
            e.tensor_scalar(gs[:, gb + 4:gb + 5], gs[:, gb + 3:gb + 4],
                            1.0, 1.0, OP.mult, OP.add)
            e.tensor_copy(self.junk[:, 0:1], gs[:, gb:gb + 1])
            e.reciprocal(gs[:, gb + 5:gb + 6], gs[:, gb + 4:gb + 5])
            e.tensor_tensor(gs[:, wfb:wfb + 8], gs[:, wfb:wfb + 8],
                            gs[:, inb:inb + 8], OP.mult)
            e.tensor_copy(self.junk[:, 0:1], gs[:, gb:gb + 1])
            v = self.sDVE.inc(
                e.tensor_scalar(gs[:, wfb:wfb + 8], gs[:, wfb:wfb + 8],
                                gs[:, gb + 5:gb + 6], None, OP.mult))
            self.mark(f"wf_{g}_{tt}", self.sDVE, v)

        @self.emit("scalar")
        def _(e, en, g=g, tt=tt, wfb=wfb):
            self.wait(e, en, f"wf_{g}_{tt}")
            v = self.sACT.inc(e.activation(self.wfbf_sb[:, :],
                                           self.gsc[:, wfb:wfb + 8], AF.Copy))
            self.mark(f"wfbf_{g}_{tt}", self.sACT, v)

        @self.emit("tensor")
        def _(e, en, g=g, tt=tt):
            self.wait(e, en, f"wfbf_{g}_{tt}")
            self.wait(e, en, "consts")
            if tt > 0:
                self.wait(e, en, f"wftevac_{g}_{tt-1}")
            elif g > 0:
                self.wait(e, en, f"wftevac_{g-1}_3")
                self.wait(e, en, f"lowc_{g-1}")
            dst = self.psD[0:8, tt * 64:tt * 64 + 64].bitcast(BF16)
            v = self.sPE.inc(e.transpose(dst, self.wfbf_sb[:, :],
                                         self.ident_sb[:, :]))
            self.mark(f"wft_{g}_{tt}", self.sPE, v)

        @self.emit("scalar")
        def _(e, en, g=g, tt=tt):
            self.wait(e, en, f"wft_{g}_{tt}")
            src = self.psD[0:8, tt * 64:tt * 64 + 64].bitcast(BF16)
            v = self.sACT.inc(e.activation(
                self.wft_sb[0:8, tt * 128:(tt + 1) * 128], src, AF.Copy))
            self.mark(f"wftevac_{g}_{tt}", self.sACT, v)

    @self.emit("tensor")
    def _(e, en, g=g):
        self.wait(e, en, f"wftevac_{g}_3")
        self.wait(e, en, f"gevac_{g}")
        v = self.sPE.inc(e.matmul(self.psC[:, :], self.rexp_sb[:, :],
                                  self.wft_sb[0:8, :], start=True, stop=True))
        self.mark(f"smatmm_{g}", self.sPE, v)

    @self.emit("scalar")
    def _(e, en, g=g):
        self.wait(e, en, f"smatmm_{g}")
        v = self.sACT.inc(e.activation(self.smat_sb[:, :], self.psC[:, :],
                                       AF.Copy))
        self.mark(f"smatevac_{g}", self.sACT, v)

    @self.emit("tensor")
    def _(e, en, g=g):
        self.wait(e, en, "la")
        self.wait(e, en, f"wftevac_{g}_3")
        if g > 0:
            self.wait(e, en, f"lowc_{g-1}")
        ins = None
        for kc in range(KC):
            ins = e.matmul(self.psD[:, :],
                           self.la_sb[:, kc * 128:(kc + 1) * 128],
                           xl(kc, 0, 512), start=(kc == 0),
                           stop=(kc == KC - 1))
        v = self.sPE.inc(ins)
        self.mark(f"lowmm_{g}", self.sPE, v)

    @self.emit("vector")
    def _(e, en, g=g):
        self.wait(e, en, f"lowmm_{g}")
        self.wait(e, en, f"smatevac_{g}")
        v = self.sDVE.inc(
            e.tensor_tensor(self.low_sb[:, :], self.psD[:, :],
                            self.smat_sb[:, :], OP.mult))
        self.mark(f"lowc_{g}", self.sDVE, v)

    for oc in range(32):
        sli = oc % 2

        @self.emit("gpsimd")
        def _(e, en, g=g, oc=oc, sli=sli):
            gi = g * 32 + oc
            if gi >= 2:
                self.wait(e, en, f"ffnmm_{(gi-2)//32}_{(gi-2)%32}")
            if oc < 2:
                self.wait(e, en, f"ln3ap_{g*4+3}")
                if g == 1:
                    self.wait(e, en, f"x2_{3}")
                    self.wait(e, en, f"store_{3}")
            gi2 = g * 32 + oc
            self.dmae(e, en, f"sW{gi2%2}",
                      self.bwsl[sli][:, :].rearrange("p (kc m) -> p kc m",
                                                     kc=KC),
                      self.hbw[oc, :, :, :], mark=f"bw_{g}_{oc}")

        @self.emit("tensor")
        def _(e, en, g=g, oc=oc, sli=sli):
            self.wait(e, en, f"bw_{g}_{oc}")
            self.wait(e, en, f"lowc_{g}")
            ps = self.psA if oc % 2 == 0 else self.psB
            if oc >= 2:
                self.wait(e, en, f"ffnevac_{g}_{oc-2}")
            elif g == 0:
                self.wait(e, en, "uevac_31_1")
            else:
                self.wait(e, en, f"ffnevac_{g-1}_31")
            for kc in range(KC):
                e.matmul(ps[:, :], self.bwsl[sli][:, kc * 128:(kc + 1) * 128],
                         xl(kc, 0, 512), start=(kc == 0), stop=False)
            ins = e.matmul(ps[:, :], self.lb_sb[:, oc * 128:(oc + 1) * 128],
                           self.low_sb[:, :], start=False, stop=True)
            v = self.sPE.inc(ins)
            self.mark(f"ffnmm_{g}_{oc}", self.sPE, v)
            if oc == 31:
                self.mark(f"ffnmm_last_{g}", self.sPE, v)

        @self.emit("scalar")
        def _(e, en, g=g, oc=oc):
            self.wait(e, en, f"ffnmm_{g}_{oc}")
            ps = self.psA if oc % 2 == 0 else self.psB
            fv = self.ffn_all[:, :].rearrange("p (ti oc t) -> p ti oc t",
                                              ti=4, oc=32)
            v = self.sACT.inc(e.activation(
                fv[:, :, oc, :],
                ps[:, :].rearrange("p (ti t) -> p ti t", ti=4),
                AF.Identity, bias=self.bb_sb[:, oc:oc + 1]))
            self.mark(f"ffnevac_{g}_{oc}", self.sACT, v)


def _ph6_stage3(self, g):
    for tt in range(4):
        ti = g * 4 + tt
        r2c, cb2c = ti * 4, ti * 4 + 1
        r4c, cb4c = ti * 4 + 2, ti * 4 + 3

        @self.emit("sync")
        def _(e, en, g=g, tt=tt, ti=ti):
            self.wait(e, en, f"ffnevac_{g}_31")
            if ti >= 2:
                self.wait(e, en, f"x2_{ti-2}")
            dst = self.ffn_tm[tt % 2][:, :].rearrange("p (oc t) -> p oc t",
                                                      oc=32)
            self.dmae(e, en, f"sTo{ti%2}", dst,
                      self.ffn_all[:, tt * D:(tt + 1) * D],
                      mark=f"ffntr_{ti}", transpose=True)

        @self.emit("scalar")
        def _(e, en, g=g, tt=tt, ti=ti):
            self.wait(e, en, f"ffntr_{ti}")
            if ti > 0:
                self.wait(e, en, f"x2_{ti-1}")
            _ln_stats(self, e, self.ffn_tm[tt % 2], ti, "ln4")

        @self.emit("vector")
        def _(e, en, ti=ti):
            self.wait(e, en, f"ln4_sums_{ti}")
            _ln_finish(self, e, "ln4", ti)

        @self.emit("scalar")
        def _(e, en, ti=ti):
            self.wait(e, en, f"ln4_var_{ti}")
            v = self.sACT.inc(
                e.activation(self.lnst[:, 55:56], self.lnst[:, 53:54],
                             AF.Sqrt))
            self.mark(f"ln4_sd_{ti}", self.sACT, v)

        @self.emit("vector")
        def _(e, en, ti=ti, r4c=r4c, cb4c=cb4c):
            _ln_rb(self, e, en, r4c, cb4c, "ln4", ti)

        @self.emit("sync")
        def _(e, en, ti=ti, g=g, tt=tt):
            self.wait(e, en, f"x1add_{g*4+3}")
            self.wait(e, en, f"ffnmm_last_{g}")
            if ti > 0:
                self.wait(e, en, f"store_{ti-1}")
            tok0 = g * 512 + tt * 128
            self.dmae(e, en, "sXa", self.xobuf[:, :],
                      self.hxo[tok0:tok0 + 128, :], mark=f"xoload2_{ti}")

        @self.emit("vector")
        def _(e, en, ti=ti, g=g, tt=tt, r2c=r2c, cb2c=cb2c, r4c=r4c,
              cb4c=cb4c):
            ln = self.lnst
            self.wait(e, en, f"ln4_rb_{ti}")
            self.wait(e, en, f"ln3ap_{g*4+3}")
            e.tensor_tensor(ln[:, 58:59], ln[:, cb2c:cb2c + 1],
                            ln[:, cb4c:cb4c + 1], OP.add)
            e.tensor_copy(self.junk[:, 0:1], ln[:, 58:59])
            e.tensor_scalar(self.x1buf[:, :], self.u_tm[ti][:, :],
                            ln[:, r2c:r2c + 1], ln[:, 58:59],
                            OP.mult, OP.add)
            self.wait(e, en, f"xoload2_{ti}")
            e.tensor_tensor(self.x1buf[:, :], self.x1buf[:, :],
                            self.xobuf[:, :], OP.add)
            self.wait(e, en, f"ffntr_{ti}")
            v = self.sDVE.inc(
                e.scalar_tensor_tensor(self.xobuf[:, :],
                                       self.ffn_tm[tt % 2][:, :],
                                       ln[:, r4c:r4c + 1],
                                       self.x1buf[:, :], OP.mult, OP.add))
            self.mark(f"x2_{ti}", self.sDVE, v)

        @self.emit("sync")
        def _(e, en, ti=ti, g=g, tt=tt):
            self.wait(e, en, f"x2_{ti}")
            tok0 = g * 512 + tt * 128
            s2 = self.sems["sSt2"]
            v = self.dmae(e, en, "sSt2", self.hout[tok0:tok0 + 128, :],
                          self.xobuf[:, :], mark=f"store_{ti}")
            if ti == 7:
                self.mark("stores_done", s2, v)


def dump_ffn(self):
    due = self.dumper("utme", [8, 128, 4096])

    @self.emit("gpsimd")
    def _(e, en):
        self.wait(e, en, "utr_7")
        s2 = self.sems["sSt2"]
        v = None
        for t in range(8):
            v = s2.inc(e.dma_start(out=due[t, :, :],
                                   in_=self.u_tm[t][:, :]), 16)
        self.mark("dump_utme", s2, v)

    du = self.dumper("utm", [4, 128, 4096])
    dx3 = self.dumper("xln3", [128, KC * 512])
    dfa = self.dumper("ffa", [128, 4 * 32 * 128])
    dln = self.dumper("lnst", [128, 64])
    dlow = self.dumper("low", [128, 512])
    dsm = self.dumper("smat", [128, 512])
    dgs = self.dumper("gsc", [128, 128])

    @self.emit("gpsimd")
    def _(e, en):
        self.wait(e, en, "stores_done")
        s2 = self.sems["sSt2"]
        for t in range(4):
            s2.inc(e.dma_start(out=du[t, :, :], in_=self.u_tm[4 + t][:, :]),
                   16)
        s2.inc(e.dma_start(out=dx3[:, :], in_=self.xln3_sb[:, :]), 16)
        s2.inc(e.dma_start(out=dfa[:, :], in_=self.ffn_all[:, :]), 16)
        s2.inc(e.dma_start(out=dln[:, :], in_=self.lnst[:, :]), 16)
        s2.inc(e.dma_start(out=dlow[:, :], in_=self.low_sb[:, :]), 16)
        s2.inc(e.dma_start(out=dsm[:, :], in_=self.smat_sb[:, :]), 16)
        v = s2.inc(e.dma_start(out=dgs[:, :], in_=self.gsc[:, :]), 16)
        self.wait(e, en, "dump_utme")
        self.mark("dump_ffn", s2, v)


def ph_final(self):
    finals = []
    if self._on("p6"):
        finals.append("stores_done")
    for nm, d in (("dump_qkv", "qkv"), ("dump_o", "o"), ("dump_u", "u"),
                  ("dump_ffn", "ffn")):
        if d in self.dumps:
            finals.append(nm)
    if not finals:
        finals.append("consts")

    for ename in ["sync", "gpsimd"]:
        @self.emit(ename)
        def _(e, en):
            for f in finals:
                self.wait(e, en, f)


for _nm in ["ph3_attention", "dump_o", "ph4_head", "ph4_exchange",
            "ph5_outproj", "dump_u", "ph6_ffn", "dump_ffn",
            "_ph6_stage1b", "_ph6_stage2", "_ph6_stage3", "ph_final"]:
    setattr(MK, _nm, globals()[_nm])


_CACHED = {}


def _get_program():
    if "nc" not in _CACHED:
        nc, mk = build_kernel(upto="p6", dumps=())
        _CACHED["nc"] = nc
        _CACHED["mk"] = mk
    return _CACHED["nc"]


def kernel(**inputs):
    from concourse.bass_utils import run_bass_kernel_spmd
    nc = _get_program()
    in_maps = [make_core_inputs(inputs, c) for c in range(8)]
    res = run_bass_kernel_spmd(nc, in_maps, core_ids=list(range(8)))
    out = np.empty((B, S, D), dtype=np.float32)
    for c in range(8):
        bb, h = c // 2, c % 2
        out[bb, h * TOK:(h + 1) * TOK] = res.results[c]["out"]
    return out


# version 11
# speedup vs baseline: 1.0071x; 1.0071x over previous
import sys
sys.path.insert(0, "/opt/trn_rl_repo")
"""Megakernel builder for mk_sandwich_d1.

Sharding (uniform SPMD program, 8 cores):
  core c: batch b=c//2, half h=c%2, owns tokens [1024h:1024h+1024],
  heads [16h:16h+16], kv heads [4h:4h+4].
Build is two-pass: pass 1 runs all emitters against mock engines to
compute every semaphore milestone; pass 2 emits for real.
"""
import math
from collections import defaultdict

import numpy as np
import ml_dtypes

import concourse.bass as bass
import concourse.mybir as mybir
from concourse.bass import IndirectOffsetOnAxis

F32 = mybir.dt.float32
BF16 = mybir.dt.bfloat16
I32 = mybir.dt.int32
AF = mybir.ActivationFunctionType
OP = mybir.AluOpType
AXX = mybir.AxisListType.X

B, S, D = 4, 2048, 4096
HD, NQ, NKV = 128, 32, 8
HQC, HKVC = 16, 4
TOK = 1024
NMC = HQC + HKVC
KC = 32
NVC = HKVC * HD
SCALE = 1.0 / math.sqrt(HD)
NEG = -1.0e30
R_LORA, N_EXP = 16, 8
LORA_SCALING = 2.0
NQT = 4

BF = np.dtype(ml_dtypes.bfloat16)

PHASES = ["p2", "p3", "p4", "p5", "p6"]


def _wtiles(w_t):
    K, M = w_t.shape
    a = w_t.reshape(KC, 128, M // 128, 128)
    a = a.transpose(2, 1, 0, 3)
    return np.ascontiguousarray(a)


def make_core_inputs(inputs, c):
    b, h = c // 2, c % 2
    x = np.asarray(inputs["L0_attn_x"], dtype=np.float32)
    w_qkv = np.asarray(inputs["L0_attn_w_qkv"], dtype=np.float32)
    b_qkv = np.asarray(inputs["L0_attn_b_qkv"], dtype=np.float32)
    w_o = np.asarray(inputs["L0_attn_w_o"], dtype=np.float32)
    b_o = np.asarray(inputs["L0_attn_b_o"], dtype=np.float32)
    gate_w = np.asarray(inputs["L0_ffn_gate_w"], dtype=np.float32)
    base_w = np.asarray(inputs["L0_ffn_base_w"], dtype=np.float32)
    base_b = np.asarray(inputs["L0_ffn_base_b"], dtype=np.float32)
    lora_A = np.asarray(inputs["L0_ffn_lora_A"], dtype=np.float32)
    lora_B = np.asarray(inputs["L0_ffn_lora_B"], dtype=np.float32)

    d = {}
    d["x"] = np.ascontiguousarray(x[b])
    d["xo"] = np.ascontiguousarray(x[b, h * TOK:(h + 1) * TOK])

    qrows = w_qkv[h * 2048:(h + 1) * 2048]
    krows = w_qkv[4096 + h * 512:4096 + (h + 1) * 512]
    wqk = np.concatenate([qrows, krows], axis=0)
    d["wqk"] = _wtiles(np.ascontiguousarray(wqk.T))
    bq = b_qkv[h * 2048:(h + 1) * 2048]
    bk = b_qkv[4096 + h * 512:4096 + (h + 1) * 512]
    d["bqk"] = np.ascontiguousarray(
        np.concatenate([bq, bk]).reshape(NMC, 128).T)

    vrows = w_qkv[5120 + h * 512:5120 + (h + 1) * 512]
    wv = vrows.T.reshape(KC, 128, NVC).transpose(1, 0, 2)
    d["wv"] = np.ascontiguousarray(wv)
    bv = b_qkv[5120 + h * 512:5120 + (h + 1) * 512]
    bvf = np.repeat(bv.reshape(HKVC, 128), 4, axis=0)
    d["bv"] = np.ascontiguousarray(bvf.T)

    my = np.arange(h * 16, h * 16 + 16)
    other = np.arange((1 - h) * 16, (1 - h) * 16 + 16)
    dorder = np.concatenate([my, other])
    wot = w_o.T.reshape(32, 128, 4096)[dorder].reshape(4096, 4096)
    d["wo"] = _wtiles(np.ascontiguousarray(wot))
    d["bo"] = np.ascontiguousarray(b_o.reshape(32, 128).T)

    d["bw"] = _wtiles(np.ascontiguousarray(base_w.T))
    d["bb"] = np.ascontiguousarray(base_b.reshape(32, 128).T)
    A_cat = lora_A.reshape(N_EXP * R_LORA, D)
    d["la"] = np.ascontiguousarray(
        A_cat.T.reshape(KC, 128, 128).transpose(1, 0, 2))
    d["lb"] = np.ascontiguousarray(
        lora_B.transpose(0, 2, 1).reshape(N_EXP * R_LORA, D))
    d["gw"] = np.ascontiguousarray(
        gate_w.T.reshape(KC, 128, N_EXP).transpose(1, 0, 2))
    rexp = np.zeros((N_EXP, 128), dtype=np.float32)
    for e in range(N_EXP):
        rexp[e, e * R_LORA:(e + 1) * R_LORA] = LORA_SCALING
    d["rexp"] = rexp.astype(BF)

    inv = 1.0 / (10000.0 ** (np.arange(0, HD, 2, dtype=np.float64) / HD))
    ang = np.arange(S, dtype=np.float64)[:, None] * inv[None, :]
    cos = np.cos(ang).T
    sin = np.sin(ang).T
    rc = np.repeat(cos, 2, axis=0).astype(np.float32)
    rs = np.empty((HD, S), dtype=np.float32)
    rs[0::2] = -sin
    rs[1::2] = sin
    d["rc"] = rc.astype(BF)
    d["rs"] = rs.astype(BF)

    perm = np.zeros((128, 128), dtype=np.float32)
    for k in range(128):
        perm[k, k ^ 1] = 1.0
    d["perm"] = perm.astype(BF)
    d["ident"] = np.eye(128, dtype=np.float32).astype(BF)

    cm = np.zeros((128, 128), dtype=np.float32)
    cm[np.triu_indices(128, 1)] = NEG
    d["cmask"] = cm

    partner = c ^ 1
    d["pidx"] = np.ascontiguousarray(
        (partner * 2048 + np.arange(2048, dtype=np.int32)).reshape(16, 128).T)
    d["selA"] = np.full((128, 1), 1.0 - h, dtype=np.float32)
    d["selB"] = np.full((128, 1), float(h), dtype=np.float32)
    return d


def _ap_span(ap):
    try:
        dims = list(ap.ap)
    except Exception:
        return None
    off = ap.offset if isinstance(ap.offset, int) else 0
    span = 1
    for st, ct in dims[1:]:
        span += (ct - 1) * st
    return (ap.tensor.name, off, off + span)


def _overlap(a, b):
    return a and b and a[0] == b[0] and a[1] < b[2] and b[1] < a[2]


class EngGuard:

    DATA_OPS = {"tensor_copy", "tensor_tensor", "tensor_add", "tensor_sub",
                "tensor_mul", "tensor_max", "tensor_scalar",
                "tensor_single_scalar", "tensor_scalar_add",
                "tensor_scalar_mul", "tensor_reduce", "reduce_max",
                "reduce_sum", "scalar_tensor_tensor", "reciprocal",
                "select", "activation", "copy", "add", "mul", "sign",
                "square", "sqrt", "memset",
                "dma_start", "dma_start_transpose"}

    def __init__(self, eng, junk_out, junk_in):
        self._eng = eng
        self._last_outs = []
        self._junk_out = junk_out
        self._junk_in = junk_in
        self._gaps = 0

    def _gap(self):
        self._eng.tensor_copy(self._junk_out, self._junk_in)             if hasattr(self._eng, "tensor_copy")             else self._eng.copy(self._junk_out, self._junk_in)
        self._gaps += 1

    def __getattr__(self, name):
        real = getattr(self._eng, name)
        if name not in self.DATA_OPS:
            return real

        def call(*args, **kw):
            from concourse.bass import AP as _AP
            aps = [a for a in args if isinstance(a, _AP)]
            for v in kw.values():
                if isinstance(v, _AP):
                    aps.append(v)
            spans = [_ap_span(a) for a in aps]
            if any(_overlap(lo, sp) for lo in self._last_outs
                   for sp in spans):
                self._gap()
            res = real(*args, **kw)
            outs = []
            if name in ("dma_start", "dma_start_transpose"):
                return res
            if args and isinstance(args[0], _AP):
                outs.append(_ap_span(args[0]))
            ao = kw.get("accum_out")
            if isinstance(ao, _AP):
                outs.append(_ap_span(ao))
            self._last_outs = [o for o in outs if o]
            return res
        return call


class _MockInst:
    def then_inc(self, sem, n=1):
        return self


class _MockEng:
    def __getattr__(self, name):
        def f(*a, **k):
            return _MockInst()
        return f


class Sem:
    def __init__(self, handle, name):
        self.h, self.name, self.v = handle, name, 0

    def inc(self, inst, n=1):
        self.v += n
        inst.then_inc(self.h, n)
        return self.v


class MK:
    def __init__(self, upto="p6", dumps=(), detect_races=False):
        self.upto = upto
        self.dumps = set(dumps)
        self.nc = bass.Bass("TRN2", target_bir_lowering=False,
                            detect_race_conditions=detect_races)
        self.streams = defaultdict(list)
        self.ev = {}
        self.wanted = set()
        self._declare_io()
        self._alloc_sbuf()
        self._alloc_psum()

    def emit(self, eng_name):
        def deco(fn):
            self.streams[eng_name].append(fn)
        return deco

    def mark(self, name, sem, val):
        if self.pass1:
            assert name not in self.ev, f"duplicate event {name}"
            self.ev[name] = (sem, val)
        else:
            assert self.ev[name] == (sem, val), (name, self.ev[name], val)

    def wait(self, e, ename, name):
        if self.pass1:
            self.wanted.add(name)
            return
        sem, val = self.ev[name]
        if val <= 0:
            return
        key = (ename, sem.name)
        if self.last_wait.get(key, -1) >= val:
            return
        e.wait_ge(sem.h, val)
        self.last_wait[key] = val

    def wait_val(self, e, ename, sem, val):
        if self.pass1 or val <= 0:
            return
        key = (ename, sem.name)
        if self.last_wait.get(key, -1) >= val:
            return
        e.wait_ge(sem.h, val)
        self.last_wait[key] = val

    def dmae(self, e, ename, lane, out, in_, mark=None, transpose=False,
             selfgate=True):
        s = self.sems[lane]
        if selfgate:
            self.wait_val(e, ename, s, s.v)
        if transpose:
            ins = e.dma_start_transpose(out=out, in_=in_)
        else:
            ins = e.dma_start(out=out, in_=in_)
        v = s.inc(ins, 16)
        if mark:
            self.mark(mark, s, v)
        return v

    def _on(self, ph):
        return PHASES.index(self.upto) >= PHASES.index(ph)

    def finalize(self):
        import contextlib
        self.stack = contextlib.ExitStack()
        self.sems = {}
        for name in ["sW0", "sW1", "sW2", "sW3", "sW4", "sXa", "sXb",
                     "sT0", "sT1", "sTp0", "sTp1", "sTo0", "sTo1",
                     "sSt0", "sSt1", "sSt2", "sCC", "sPE", "sACT", "sDVE"]:
            self.sems[name] = Sem(
                self.stack.enter_context(self.nc.semaphore(name)), name)
        self.__dict__.update(self.sems)

        self._build_phases()

        self.pass1 = True
        self.last_wait = {}
        mocks = {k: _MockEng() for k in
                 ["sync", "scalar", "vector", "tensor", "gpsimd"]}
        for ename, fns in self.streams.items():
            for fn in fns:
                fn(mocks[ename], ename)
        missing = self.wanted - set(self.ev)
        assert not missing, f"unknown events waited on: {sorted(missing)[:10]}"
        snap = {}
        for s in self.sems.values():
            snap[s.name] = s.v
            s.v = 0
        self.pass1 = False
        self.last_wait = {}
        self.ev2 = dict(self.ev)
        with self.nc.Block() as block:
            self.guards = {}
            for ename in ["sync", "scalar", "vector", "tensor", "gpsimd"]:
                def runner(e, ename=ename):
                    if ename == "vector":
                        e = EngGuard(e, self.junk[:, 2:3], self.junk[:, 3:4])
                        self.guards[ename] = e
                    elif ename == "scalar":
                        e = EngGuard(e, self.junk[:, 4:5], self.junk[:, 5:6])
                        self.guards[ename] = e
                    for fn in self.streams[ename]:
                        fn(e, ename)
                getattr(block, ename)(runner)
        for en, g in self.guards.items():
            print(f"guard[{en}]: {g._gaps} gap ops inserted")
        for s in self.sems.values():
            assert s.v == snap[s.name], (s.name, s.v, snap[s.name])
        self.stack.close()
        return self.nc

    def _declare_io(self):
        nc = self.nc
        EI = dict(kind="ExternalInput")
        self.hx = nc.dram_tensor("x", [S, D], F32, **EI)
        self.hxo = nc.dram_tensor("xo", [TOK, D], F32, **EI)
        self.hwqk = nc.dram_tensor("wqk", [NMC, 128, KC, 128], F32, **EI)
        self.hbqk = nc.dram_tensor("bqk", [128, NMC], F32, **EI)
        self.hwv = nc.dram_tensor("wv", [128, KC, NVC], F32, **EI)
        self.hbv = nc.dram_tensor("bv", [128, HQC], F32, **EI)
        self.hwo = nc.dram_tensor("wo", [32, 128, 32, 128], F32, **EI)
        self.hbo = nc.dram_tensor("bo", [128, 32], F32, **EI)
        self.hbw = nc.dram_tensor("bw", [32, 128, 32, 128], F32, **EI)
        self.hbb = nc.dram_tensor("bb", [128, 32], F32, **EI)
        self.hla = nc.dram_tensor("la", [128, KC, 128], F32, **EI)
        self.hlb = nc.dram_tensor("lb", [128, D], F32, **EI)
        self.hgw = nc.dram_tensor("gw", [128, KC, N_EXP], F32, **EI)
        self.hrexp = nc.dram_tensor("rexp", [N_EXP, 128], BF16, **EI)
        self.hrc = nc.dram_tensor("rc", [128, S], BF16, **EI)
        self.hrs = nc.dram_tensor("rs", [128, S], BF16, **EI)
        self.hperm = nc.dram_tensor("perm", [128, 128], BF16, **EI)
        self.hident = nc.dram_tensor("ident", [128, 128], BF16, **EI)
        self.hcmask = nc.dram_tensor("cmask", [128, 128], F32, **EI)
        self.hpidx = nc.dram_tensor("pidx", [128, 16], I32, **EI)
        self.hselA = nc.dram_tensor("selA", [128, 1], F32, **EI)
        self.hselB = nc.dram_tensor("selB", [128, 1], F32, **EI)
        self.hout = nc.dram_tensor("out", [TOK, D], F32, kind="ExternalOutput")
        self.cc_in = nc.dram_tensor("cc_in", [HQC * 128, TOK], BF16)
        self.cc_out = nc.dram_tensor("cc_out", [8 * HQC * 128, TOK], BF16)
        self.dump_t = {}

    def dumper(self, name, shape, dtype=F32):
        t = self.nc.dram_tensor("d_" + name, shape, dtype,
                                kind="ExternalOutput")
        self.dump_t[name] = t
        return t

    def _alloc_sbuf(self):
        nc = self.nc
        self.arena = nc.alloc_sbuf_tensor("arena", [128, 212448],
                                          mybir.dt.uint8)
        def at(name, shape, dtype, off):
            return nc.alloc_sbuf_tensor_at(name, shape, dtype, offset=off)
        SB = 16512
        KA, KB, KCr, KD, KE, KF = (SB + 0, SB + 65536, SB + 81920,
                                   SB + 98304, SB + 131072, SB + 196608)
        self.qo = [at(f"qo{i}", [128, 2048], BF16, KA + i * 4096)
                   for i in range(16)]
        self.wosl = [at(f"wosl{i}", [128, KC * 128], BF16, KA + i * 8192)
                     for i in range(2)]
        self.u_tm = [at(f"utm{i}", [128, 4096], BF16, KA + i * 8192)
                     for i in range(8)]
        self.Ksb = [at(f"K{i}", [128, 2048], BF16, KB + i * 4096)
                    for i in range(4)]
        self.la_sb = at("la_sb", [128, KC * 128], BF16, KB)
        self.lb_sb = at("lb_sb", [128, D], BF16, KB + 8192)
        self.v_sb = at("v_sb", [128, 16 * NVC], BF16, KCr)
        self.ffn_tm = [at(f"ffntm{i}", [128, 4096], BF16, KCr + i * 8192)
                       for i in range(2)]
        self.o_oth = [at(f"ooth{i}", [128, 1024], BF16, KB + i * 2048)
                      for i in range(16)]
        self.xln0 = at("xln0", [128, KC * 512], BF16, KD)
        self.o_sel = [at(f"osel{i}", [128, 1024], BF16, KD + i * 2048)
                      for i in range(16)]
        self.x1buf = at("x1buf", [128, 4096], F32, KD)
        self.xobuf = at("xobuf", [128, 4096], F32, KD + 16384)
        self.bwsl = [at(f"bwsl{i}", [128, KC * 128], BF16, KD + i * 16384)
                     for i in range(2)]
        self.xt = at("xt", [128, 4096], F32, KE)
        self.xln_tm = at("xln_tm", [128, 4096], BF16, KE + 16384)
        self.wqksl = [at(f"wqksl{i}", [128, KC * 128], BF16,
                         KE + 24576 + i * 8192) for i in range(2)]
        self.xln1a = at("xln1a", [128, 24 * 512], BF16, KE + 40960)
        self.u_all = at("u_all", [128, 8 * 32 * 128], BF16, KE)
        self.xln3_sb = at("xln3_sb", [128, KC * 512], BF16, KE)
        self.ffn_all = at("ffn_all", [128, 4 * 32 * 128], BF16, KE + 32768)
        self.rc_sb = at("rc_sb", [128, S], BF16, KE)
        self.rs_sb = at("rs_sb", [128, S], BF16, KE + 4096)
        self.p_scr = [at(f"p{i}", [128, 1024], BF16, KE + 8192 + i * 2048)
                      for i in range(2)]
        self.pt_scr = [at(f"pt{i}", [128, 1024], BF16, KE + 12288 + i * 2048)
                       for i in range(2)]
        self.cm_sb = at("cm_sb", [128, 128], F32, KE + 16384)
        self.perm_sb = at("perm_sb", [128, 128], BF16, KE + 16896)
        self.otmp = [at(f"otmp{i}", [128, 128], BF16, KE + 17152 + i * 256)
                     for i in range(2)]
        self.t1_scr = [at(f"t1s{i}", [128, 512], BF16, KE + 17664 + i * 1024)
                       for i in range(2)]
        self.xln1b = at("xln1b", [128, 8 * 512], BF16, KF)
        self.xstage = at("xstage", [128, 4096], BF16, KF)
        FO = KF + 8192
        def nxt(name, shape, dtype, size):
            nonlocal FO
            t = at(name, shape, dtype, FO)
            FO += size
            return t
        self.bqk_sb = nxt("bqk_sb", [128, NMC], F32, 96)
        self.bv_sb = nxt("bv_sb", [128, HQC], F32, 64)
        self.bo_sb = nxt("bo_sb", [128, 32], F32, 128)
        self.bb_sb = nxt("bb_sb", [128, 32], F32, 128)
        self.ident_sb = nxt("ident_sb", [128, 128], BF16, 256)
        self.gw_sb = nxt("gw_sb", [128, KC * N_EXP], BF16, 512)
        self.rexp_sb = nxt("rexp_sb", [N_EXP, 128], BF16, 256)
        self.pidx_sb = nxt("pidx_sb", [128, 16], I32, 64)
        self.selA_sb = nxt("selA_sb", [128, 1], F32, 32)
        self.selB_sb = nxt("selB_sb", [128, 1], F32, 32)
        self.wvsl = [nxt(f"wvsl{i}", [128, NVC], BF16, 1024) for i in range(2)]
        self.ast = nxt("ast", [128, 32], F32, 128)
        self.lnst = nxt("lnst", [128, 64], F32, 256)
        self.gsc = nxt("gsc", [128, 128], F32, 512)
        self.low_sb = nxt("low_sb", [128, 512], BF16, 1024)
        self.smat_sb = nxt("smat_sb", [128, 512], BF16, 1024)
        self.wft_sb = nxt("wft_sb", [8, 512], BF16, 1024)
        self.wfbf_sb = nxt("wfbf_sb", [128, 8], BF16, 32)
        self.junk = nxt("junk", [128, 8], F32, 32)
        assert FO <= 16512 + 212448, FO

    def _alloc_psum(self):
        nc = self.nc
        self.ps4 = nc.alloc_psum_tensor("ps4", [128, 2048], F32)
        self.psA = nc.alloc_psum_tensor("psA", [128, 512], F32)
        self.psB = nc.alloc_psum_tensor("psB", [128, 512], F32)
        self.psC = nc.alloc_psum_tensor("psC", [128, 512], F32)
        self.psD = nc.alloc_psum_tensor("psD", [128, 512], F32)

    def _build_phases(self):
        self.ph0_consts()
        self.ph12_qkv()
        if "qkv" in self.dumps:
            self.dump_qkv()
        if self._on("p3"):
            self.ph3_attention()
            if "o" in self.dumps:
                self.dump_o()
        if self._on("p4"):
            self.ph4_exchange()
        if self._on("p5"):
            self.ph5_outproj()
            if "u" in self.dumps:
                self.dump_u()
        if self._on("p6"):
            self.ph6_ffn()
            if "ffn" in self.dumps:
                self.dump_ffn()
        self.ph_final()

    def ph0_consts(self):
        @self.emit("sync")
        def _(e, en):
            v = 0
            for dst, src in [(self.bqk_sb, self.hbqk), (self.bv_sb, self.hbv),
                             (self.bo_sb, self.hbo), (self.bb_sb, self.hbb),
                             (self.ident_sb, self.hident),
                             (self.rexp_sb, self.hrexp),
                             (self.pidx_sb, self.hpidx),
                             (self.selA_sb, self.hselA),
                             (self.selB_sb, self.hselB)]:
                v = self.sXb.inc(e.dma_start(out=dst[:, :], in_=src[:, :]),
                                 16)
            self.mark("consts", self.sXb, v)

        @self.emit("gpsimd")
        def _(e, en):
            self.dmae(e, en, "sW4",
                      self.gw_sb[:, :].rearrange("p (kc x) -> p kc x", kc=KC),
                      self.hgw[:, :, :], mark="gw")

        @self.emit("vector")
        def _(e, en):
            e.memset(self.junk[:, :], 0.0)

        @self.emit("scalar")
        def _(e, en):
            e.memzero(self.junk[:, 4:8])

    def xln_ap(self, q, kc, c0, cn):
        if q % 2 == 0:
            return self.xln0[:, kc * 512 + c0: kc * 512 + c0 + cn]
        if kc < 24:
            return self.xln1a[:, kc * 512 + c0: kc * 512 + c0 + cn]
        return self.xln1b[:, (kc - 24) * 512 + c0: (kc - 24) * 512 + c0 + cn]

    def ph12_qkv(self):
        for q in range(NQT):
            for tt in range(4):
                self._ln1_tile(q, tt)
            self._msweep(q)
            self._vprod(q)

    def _ln1_tile(self, q, tt):
        ti = q * 4 + tt
        tok0 = q * 512 + tt * 128

        @self.emit("sync")
        def _(e, en, ti=ti, tok0=tok0):
            if ti > 0:
                self.wait(e, en, f"ln1_apply_{ti-1}")
            self.dmae(e, en, "sXa", self.xt[:, :],
                      self.hx[tok0:tok0 + 128, :], mark=f"xload_{ti}")

        @self.emit("scalar")
        def _(e, en, ti=ti, q=q, tt=tt):
            self.wait(e, en, f"xload_{ti}")
            a = self.ast
            v = None
            sA = self.sems["sACT"]
            for i in range(4):
                sl = self.xt[:, i * 1024:(i + 1) * 1024]
                if v is not None:
                    self.wait_val(e, en, sA, v)
                e.activation(self.ps4[:, 0:1024], sl, AF.Square,
                             accum_out=a[:, 8 + i:9 + i])
                v = self.sACT.inc(
                    e.activation(self.ps4[:, 1024:2048], sl, AF.Copy,
                                 accum_out=a[:, 12 + i:13 + i]))
            self.mark(f"ln1_sums_{ti}", self.sACT, v)

        @self.emit("vector")
        def _(e, en, ti=ti):
            self.wait(e, en, f"ln1_sums_{ti}")
            a = self.ast
            e.tensor_tensor(a[:, 2:3], a[:, 8:9], a[:, 9:10], OP.add)
            e.tensor_tensor(a[:, 3:4], a[:, 12:13], a[:, 13:14], OP.add)
            e.tensor_tensor(a[:, 16:17], a[:, 10:11], a[:, 11:12], OP.add)
            e.tensor_tensor(a[:, 17:18], a[:, 14:15], a[:, 15:16], OP.add)
            e.tensor_tensor(a[:, 2:3], a[:, 2:3], a[:, 16:17], OP.add)
            e.tensor_tensor(a[:, 3:4], a[:, 3:4], a[:, 17:18], OP.add)
            e.tensor_scalar(a[:, 4:5], a[:, 3:4], 1.0 / D, None, OP.mult)
            e.tensor_scalar(a[:, 5:6], a[:, 2:3], 1.0 / D, None, OP.mult)
            e.tensor_tensor(a[:, 6:7], a[:, 4:5], a[:, 4:5], OP.mult)
            e.tensor_tensor(a[:, 5:6], a[:, 5:6], a[:, 6:7], OP.subtract)
            v = self.sDVE.inc(
                e.tensor_scalar(a[:, 5:6], a[:, 5:6], 1.0, 1e-5,
                                OP.mult, OP.add))
            self.mark(f"ln1_var_{ti}", self.sDVE, v)

        @self.emit("scalar")
        def _(e, en, ti=ti):
            self.wait(e, en, f"ln1_var_{ti}")
            v = self.sACT.inc(
                e.activation(self.ast[:, 7:8], self.ast[:, 5:6], AF.Sqrt))
            self.mark(f"ln1_sd_{ti}", self.sACT, v)

        @self.emit("vector")
        def _(e, en, ti=ti):
            self.wait(e, en, f"ln1_sd_{ti}")
            a = self.ast
            e.reciprocal(a[:, 0:1], a[:, 7:8])
            e.tensor_copy(self.junk[:, 0:1], a[:, 4:5])
            e.tensor_tensor(a[:, 1:2], a[:, 4:5], a[:, 0:1], OP.mult)
            v = self.sDVE.inc(
                e.tensor_scalar(a[:, 1:2], a[:, 1:2], -1.0, None, OP.mult))
            self.mark(f"ln1_rb_{ti}", self.sDVE, v)

        @self.emit("scalar")
        def _(e, en, ti=ti, q=q, tt=tt):
            a = self.ast
            self.wait(e, en, f"ln1_rb_{ti}")
            v = self.sACT.inc(
                e.activation(self.xln_tm[:, :], self.xt[:, :], AF.Identity,
                             scale=a[:, 0:1], bias=a[:, 1:2]))
            self.mark(f"ln1_apply_{ti}", self.sACT, v)

        @self.emit("sync")
        def _(e, en, ti=ti, q=q, tt=tt):
            self.wait(e, en, f"ln1_apply_{ti}")
            if q >= 2:
                self.wait(e, en, f"vmm_last_{q-2}")
            lane = "sT0" if q % 2 == 0 else "sT1"

            if q % 2 == 0:
                dst = self.xln0[:, :].rearrange("p (kc t) -> p kc t", kc=KC)
                self.dmae(e, en, lane, dst[:, :, tt * 128:(tt + 1) * 128],
                          self.xln_tm[:, :], mark=f"xln_tr_{ti}",
                          transpose=True, selfgate=False)
            else:
                d1 = self.xln1a[:, :].rearrange("p (kc t) -> p kc t", kc=24)
                d2 = self.xln1b[:, :].rearrange("p (kc t) -> p kc t", kc=8)
                self.dmae(e, en, lane, d1[:, :, tt * 128:(tt + 1) * 128],
                          self.xln_tm[:, 0:24 * 128], transpose=True,
                          selfgate=False)
                self.dmae(e, en, lane, d2[:, :, tt * 128:(tt + 1) * 128],
                          self.xln_tm[:, 24 * 128:], mark=f"xln_tr_{ti}",
                          transpose=True, selfgate=False)


    def _msweep(self, q):
        for mc in range(NMC):
            gi = q * NMC + mc
            sli = gi % 2

            @self.emit("gpsimd")
            def _(e, en, q=q, mc=mc, gi=gi, sli=sli):
                if gi >= 2:
                    self.wait(e, en, f"qk_mm_{gi-2}")
                self.dmae(e, en, f"sW{gi%2}",
                          self.wqksl[sli][:, :].rearrange(
                              "p (kc m) -> p kc m", kc=KC),
                          self.hwqk[mc, :, :, :], mark=f"wqk_{gi}")

            @self.emit("tensor")
            def _(e, en, q=q, mc=mc, gi=gi, sli=sli):
                self.wait(e, en, f"wqk_{gi}")
                self.wait(e, en, f"xln_tr_{q*4+3}")
                if gi >= 2:
                    self.wait(e, en, f"qk_evac_{gi-2}")
                if mc < 2 and q > 0:
                    self.wait(e, en, f"v_evac_{q-1}_1")
                ps = self.psA if gi % 2 == 0 else self.psB
                ins = None
                for kc in range(KC):
                    ins = e.matmul(ps[:, :],
                                   self.wqksl[sli][:, kc * 128:(kc + 1) * 128],
                                   self.xln_ap(q, kc, 0, 512),
                                   start=(kc == 0), stop=(kc == KC - 1))
                v = self.sPE.inc(ins)
                self.mark(f"qk_mm_{gi}", self.sPE, v)

            @self.emit("scalar")
            def _(e, en, q=q, mc=mc, gi=gi):
                self.wait(e, en, f"qk_mm_{gi}")
                self.wait(e, en, "consts")
                ps = self.psA if gi % 2 == 0 else self.psB
                if mc < HQC:
                    dst = self.qo[mc][:, q * 512:(q + 1) * 512]
                else:
                    dst = self.Ksb[mc - HQC][:, q * 512:(q + 1) * 512]
                v = self.sACT.inc(e.activation(dst, ps[:, :], AF.Identity,
                                               bias=self.bqk_sb[:, mc:mc + 1]))
                self.mark(f"qk_evac_{gi}", self.sACT, v)

    def _vprod(self, q):
        for kc in range(KC):
            gi = q * KC + kc
            sli = gi % 2

            @self.emit("gpsimd")
            def _(e, en, gi=gi, kc=kc, sli=sli):
                if gi >= 2:
                    self.wait(e, en, f"v_mm_{gi-2}")
                self.dmae(e, en, f"sW{2+gi%2}", self.wvsl[sli][:, :],
                          self.hwv[:, kc, :], mark=f"wv_{gi}")

            @self.emit("tensor")
            def _(e, en, q=q, kc=kc, gi=gi, sli=sli):
                self.wait(e, en, f"wv_{gi}")
                if kc == 0:
                    self.wait(e, en, f"qk_evac_{q*NMC+NMC-1}")
                    if q > 0:
                        self.wait(e, en, f"v_evac_{q-1}_3")
                ins = None
                for tt, ps in enumerate([self.psA, self.psB,
                                         self.psC, self.psD]):
                    ins = e.matmul(ps[:, :],
                                   self.xln_ap(q, kc, tt * 128, 128),
                                   self.wvsl[sli][:, :],
                                   start=(kc == 0), stop=(kc == KC - 1))
                v = self.sPE.inc(ins)
                self.mark(f"v_mm_{gi}", self.sPE, v)
                if kc == KC - 1:
                    self.mark(f"vmm_last_{q}", self.sPE, v)

        for tt in range(4):
            @self.emit("scalar")
            def _(e, en, q=q, tt=tt):
                self.wait(e, en, f"v_mm_{q*KC+KC-1}")
                ps = [self.psA, self.psB, self.psC, self.psD][tt]
                v = self.sACT.inc(e.activation(
                    self.v_sb[:, (q * 4 + tt) * NVC:(q * 4 + tt + 1) * NVC],
                    ps[:, :], AF.Copy))
                self.mark(f"v_evac_{q}_{tt}", self.sACT, v)

    def dump_qkv(self):
        dq = self.dumper("q", [HQC, 128, S])
        dk = self.dumper("k", [HKVC, 128, S])
        dv = self.dumper("v", [S, NVC])

        @self.emit("gpsimd")
        def _(e, en):
            self.wait(e, en, f"qk_evac_{NQT*NMC-1}")
            for t in range(4):
                self.wait(e, en, f"v_evac_{NQT-1}_{t}")
            v = 0
            for i in range(HQC):
                v = self.sSt2.inc(e.dma_start(out=dq[i, :, :],
                                              in_=self.qo[i][:, :]), 16)
            for i in range(HKVC):
                v = self.sSt2.inc(e.dma_start(out=dk[i, :, :],
                                              in_=self.Ksb[i][:, :]), 16)
            v = self.sSt2.inc(e.dma_start(
                out=dv[:, :].rearrange("(a p) x -> p a x", p=128),
                in_=self.v_sb[:, :].rearrange("p (a x) -> p a x", a=16)), 16)
            self.mark("dump_qkv", self.sSt2, v)

    pass


def build_kernel(upto="p6", dumps=(), detect_races=False):
    mk = MK(upto=upto, dumps=dumps, detect_races=detect_races)
    nc = mk.finalize()
    return nc, mk





HD = 128
HQC, HKVC = 16, 4
KC = 32
NVC = 512
NMC = 20
NQT = 4
SCALE = 1.0 / math.sqrt(HD)
S = 2048
TOK = 1024
D = 4096
N_EXP = 8


def _pu_cols(pu):
    b = 8 + pu * 5
    return dict(nmraw=b, tmp=b + 1, farg=b + 2, f=b + 3, lsub=b + 4)


def _po_cols(po):
    b = 18 + po * 3
    return dict(mrun=b, lrun=b + 1, rl=b + 2)


def _units():
    us = []
    for hh in range(HQC):
        for qt in range(16):
            klen = (qt + 1) * 128
            subs = [(0, min(klen, 1024))]
            if klen > 1024:
                subs.append((1024, klen - 1024))
            for si, (koff, kl) in enumerate(subs):
                us.append(dict(hh=hh, qt=qt, si=si, koff=koff, kl=kl,
                               last=(si == len(subs) - 1)))
    for i, u in enumerate(us):
        u["i"] = i
    return us


def ph3_attention(self):
    units = _units()
    a = self.ast

    @self.emit("sync")
    def _(e, en):
        self.wait(e, en, "vmm_last_3")
        self.sXb.inc(e.dma_start(out=self.rc_sb[:, :], in_=self.hrc[:, :]),
                     16)
        self.sXb.inc(e.dma_start(out=self.cm_sb[:, :], in_=self.hcmask[:, :]),
                     16)
        self.sXb.inc(e.dma_start(out=self.perm_sb[:, :],
                                 in_=self.hperm[:, :]), 16)
        v = self.sXb.inc(e.dma_start(out=self.rs_sb[:, :],
                                     in_=self.hrs[:, :]), 16)
        self.mark("ropetabs", self.sXb, v)

    self._rope_n = 0

    def rope_buf(name, buf, evac_ev):
        for cb in range(4):
            ri = self._rope_n
            self._rope_n += 1
            ps = self.psC if ri % 2 == 0 else self.psD

            @self.emit("tensor")
            def _(e, en, ri=ri, cb=cb, buf=buf, ps=ps, evac_ev=evac_ev):
                self.wait(e, en, evac_ev)
                self.wait(e, en, "ropetabs")
                if ri < 2:
                    self.wait(e, en, "v_evac_3_2")
                    self.wait(e, en, "v_evac_3_3")
                if ri >= 2:
                    self.wait(e, en, f"rope_dve_{ri-2}")
                v = self.sPE.inc(e.matmul(ps[:, :], self.perm_sb[:, :],
                                          buf[:, cb * 512:(cb + 1) * 512],
                                          start=True, stop=True))
                self.mark(f"rope_mm_{ri}", self.sPE, v)

            @self.emit("vector")
            def _(e, en, ri=ri, cb=cb, buf=buf, ps=ps, name=name):
                self.wait(e, en, f"rope_mm_{ri}")
                self.wait(e, en, "ropetabs")
                t1 = self.t1_scr[ri % 2]
                c0, c1 = cb * 512, (cb + 1) * 512
                e.tensor_tensor(t1[:, :], buf[:, c0:c1], self.rc_sb[:, c0:c1],
                                OP.mult)
                e.tensor_tensor(buf[:, c0:c1], ps[:, :], self.rs_sb[:, c0:c1],
                                OP.mult)
                v = self.sDVE.inc(
                    e.tensor_tensor(buf[:, c0:c1], buf[:, c0:c1], t1[:, :],
                                    OP.add))
                self.mark(f"rope_dve_{ri}", self.sDVE, v)
                if cb == 3:
                    self.mark(f"rope_done_{name}", self.sDVE, v)

    for i in range(HKVC):
        rope_buf(f"k{i}", self.Ksb[i], f"qk_evac_{3*NMC + HQC + i}")

    def emit_S(u):
        i, hh, qt = u["i"], u["hh"], u["qt"]
        pu = i % 2

        @self.emit("tensor")
        def _(e, en, u=u, i=i, hh=hh, qt=qt, pu=pu):
            self.wait(e, en, f"rope_done_q{hh}")
            self.wait(e, en, f"rope_done_k{hh//4}")
            if i >= 2:
                self.wait(e, en, f"exp_{i-2}")
            ins = None
            for kb in range(0, u["kl"], 512):
                nb = min(512, u["kl"] - kb)
                ins = e.matmul(
                    self.ps4[:, pu * 1024 + kb: pu * 1024 + kb + nb],
                    self.qo[hh][:, qt * 128:(qt + 1) * 128],
                    self.Ksb[hh // 4][:, u["koff"] + kb: u["koff"] + kb + nb],
                    start=True, stop=True)
            v = self.sPE.inc(ins)
            self.mark(f"smm_{i}", self.sPE, v)

    def emit_softmax(u):
        i, qt = u["i"], u["qt"]
        pu = i % 2
        po = (u["hh"] * 16 + qt) % 2
        pc, ocl = _pu_cols(pu), _po_cols(po)

        @self.emit("vector")
        def _(e, en, u=u, i=i, pu=pu, pc=pc, ocl=ocl):
            self.wait(e, en, f"smm_{i}")
            self.wait(e, en, "ropetabs")
            base = pu * 1024
            if u["koff"] + u["kl"] == (u["qt"] + 1) * 128:
                off = base + u["kl"] - 128
                e.tensor_tensor(self.ps4[:, off:off + 128],
                                self.ps4[:, off:off + 128],
                                self.cm_sb[:, :], OP.add)
            e.tensor_reduce(a[:, pc["nmraw"]:pc["nmraw"] + 1],
                            self.ps4[:, base:base + u["kl"]],
                            AXX, OP.max, negate=True)
            if u["si"] == 0:
                v = self.sDVE.inc(
                    e.tensor_scalar(a[:, ocl["mrun"]:ocl["mrun"] + 1],
                                    a[:, pc["nmraw"]:pc["nmraw"] + 1],
                                    SCALE, None, OP.mult))
            else:
                e.tensor_scalar(a[:, pc["tmp"]:pc["tmp"] + 1],
                                a[:, pc["nmraw"]:pc["nmraw"] + 1],
                                SCALE, None, OP.mult)
                e.tensor_tensor(a[:, pc["tmp"]:pc["tmp"] + 1],
                                a[:, pc["tmp"]:pc["tmp"] + 1],
                                a[:, ocl["mrun"]:ocl["mrun"] + 1], OP.min)
                e.tensor_tensor(a[:, pc["farg"]:pc["farg"] + 1],
                                a[:, pc["tmp"]:pc["tmp"] + 1],
                                a[:, ocl["mrun"]:ocl["mrun"] + 1], OP.subtract)
                v = self.sDVE.inc(
                    e.tensor_copy(a[:, ocl["mrun"]:ocl["mrun"] + 1],
                                  a[:, pc["tmp"]:pc["tmp"] + 1]))
            self.mark(f"sfx_{i}", self.sDVE, v)

        @self.emit("scalar")
        def _(e, en, u=u, i=i, pu=pu, pc=pc, ocl=ocl):
            self.wait(e, en, f"sfx_{i}")
            if i >= 2:
                self.wait(e, en, f"ptr_{i-2}")
            base = pu * 1024
            v = self.sACT.inc(
                e.activation(self.p_scr[pu][:, 0:u["kl"]],
                             self.ps4[:, base:base + u["kl"]], AF.Exp,
                             scale=SCALE,
                             bias=a[:, ocl["mrun"]:ocl["mrun"] + 1],
                             accum_out=a[:, pc["lsub"]:pc["lsub"] + 1]))
            self.mark(f"exp_{i}", self.sACT, v)
            if u["si"] == 1:
                v = self.sACT.inc(
                    e.activation(a[:, pc["f"]:pc["f"] + 1],
                                 a[:, pc["farg"]:pc["farg"] + 1], AF.Exp))
                self.mark(f"fexp_{i}", self.sACT, v)

        @self.emit("sync")
        def _(e, en, u=u, i=i, pu=pu):
            if i >= 2:
                self.wait(e, en, f"omm_{i-2}")
            self.wait(e, en, f"exp_{i}")
            nt = u["kl"] // 128
            dst = self.pt_scr[pu][:, 0:u["kl"]].rearrange(
                "p (t q) -> p t q", t=nt)
            self.dmae(e, en, f"sTp{i%2}", dst,
                      self.p_scr[pu][:, 0:u["kl"]], mark=f"ptr_{i}",
                      transpose=True)

        @self.emit("vector")
        def _(e, en, u=u, i=i, pu=pu, pc=pc, ocl=ocl, po=po):
            if u["si"] == 0:
                self.wait(e, en, f"exp_{i}")
                v = self.sDVE.inc(
                    e.tensor_copy(a[:, ocl["lrun"]:ocl["lrun"] + 1],
                                  a[:, pc["lsub"]:pc["lsub"] + 1]))
                self.mark(f"lupd_{i}", self.sDVE, v)
            else:
                self.wait(e, en, f"fexp_{i}")
                self.wait(e, en, f"exp_{i}")
                self.wait(e, en, f"omm_{i-1}")
                ps = self.psA if po == 0 else self.psB
                e.tensor_scalar(ps[:, 0:128], ps[:, 0:128],
                                a[:, pc["f"]:pc["f"] + 1], None, OP.mult)
                v = self.sDVE.inc(
                    e.scalar_tensor_tensor(
                        a[:, ocl["lrun"]:ocl["lrun"] + 1],
                        a[:, ocl["lrun"]:ocl["lrun"] + 1],
                        a[:, pc["f"]:pc["f"] + 1],
                        a[:, pc["lsub"]:pc["lsub"] + 1],
                        OP.mult, OP.add))
                self.mark(f"lupd_{i}", self.sDVE, v)

    def emit_O(u):
        i, hh, qt = u["i"], u["hh"], u["qt"]
        pu = i % 2
        po = (hh * 16 + qt) % 2
        oi = hh * 16 + qt
        kv = hh // 4
        ocl = _po_cols(po)

        @self.emit("tensor")
        def _(e, en, u=u, i=i, pu=pu, po=po, oi=oi, kv=kv):
            self.wait(e, en, f"ptr_{i}")
            if u["si"] == 1:
                self.wait(e, en, f"lupd_{i}")
            if u["si"] == 0 and oi >= 2:
                self.wait(e, en, f"oevac_{oi-2}")
            ps = self.psA if po == 0 else self.psB
            ins = None
            nt = u["kl"] // 128
            for kt in range(nt):
                tok = u["koff"] + kt * 128
                ti = tok // 128
                vcol = ti * NVC + kv * 128
                ins = e.matmul(
                    ps[:, 0:128],
                    self.pt_scr[pu][:, kt * 128:(kt + 1) * 128],
                    self.v_sb[:, vcol:vcol + 128],
                    start=(u["si"] == 0 and kt == 0),
                    stop=(kt == nt - 1),
                    skip_group_check=(u["si"] == 1))
            v = self.sPE.inc(ins)
            self.mark(f"omm_{i}", self.sPE, v)

        if not u["last"]:
            return

        @self.emit("vector")
        def _(e, en, i=i, ocl=ocl, oi=oi):
            self.wait(e, en, f"omm_{i}")
            e.tensor_copy(self.junk[:, 0:1], a[:, 0:1])
            v = self.sDVE.inc(
                e.reciprocal(a[:, ocl["rl"]:ocl["rl"] + 1],
                             a[:, ocl["lrun"]:ocl["lrun"] + 1]))
            self.mark(f"rl_{oi}", self.sDVE, v)

        @self.emit("scalar")
        def _(e, en, i=i, po=po, oi=oi, ocl=ocl, hh=hh, qt=qt):
            self.wait(e, en, f"omm_{i}")
            self.wait(e, en, f"rl_{oi}")
            if oi >= 2:
                self.wait(e, en, f"oevac_{oi-2}")
            ps = self.psA if po == 0 else self.psB
            v = self.sACT.inc(
                e.activation(self.otmp[po][:, :], ps[:, 0:128], AF.Identity,
                             scale=a[:, ocl["rl"]:ocl["rl"] + 1]))
            self.mark(f"oevac_a_{oi}", self.sACT, v)

        @self.emit("sync")
        def _(e, en, po=po, oi=oi, hh=hh, qt=qt):
            self.wait(e, en, f"oevac_a_{oi}")
            dst = self.qo[hh][:, qt * 128:(qt + 1) * 128].rearrange(
                "p (t q) -> p t q", t=1)
            self.dmae(e, en, f"sTo{oi%2}", dst, self.otmp[po][:, :],
                      mark=f"oevac_{oi}", transpose=True)

    prev = None
    for u in units:
        if u["qt"] == 0 and u["si"] == 0:
            rope_buf(f"q{u['hh']}", self.qo[u["hh"]],
                     f"qk_evac_{3*NMC + u['hh']}")
        emit_S(u)
        emit_softmax(u)
        if prev is not None:
            emit_O(prev)
            if prev["last"] and prev["qt"] == 15 and self._on("p4"):
                self.ph4_head(prev["hh"])
        prev = u
    emit_O(prev)
    if self._on("p4"):
        self.ph4_head(prev["hh"])


def dump_o(self):
    do = self.dumper("o", [HQC, 128, S])

    @self.emit("gpsimd")
    def _(e, en):
        v = 0
        for hh in range(HQC):
            self.wait(e, en, f"oevac_{hh*16+15}")
            v = self.sSt2.inc(e.dma_start(out=do[hh, :, :],
                                          in_=self.qo[hh][:, :]), 16)
        self.mark("dump_o", self.sSt2, v)


def ph4_head(self, hh):
    @self.emit("vector")
    def _(e, en, hh=hh):
        self.wait(e, en, f"oevac_{hh*16+15}")
        q = self.qo[hh]
        osl = self.o_sel[hh]
        e.tensor_scalar(osl[:, :], q[:, 0:1024], self.selB_sb[:, 0:1],
                        self.bv_sb[:, hh:hh + 1], OP.mult, OP.add)
        v = self.sDVE.inc(
            e.scalar_tensor_tensor(osl[:, :], q[:, 1024:2048],
                                   self.selA_sb[:, 0:1], osl[:, :],
                                   OP.mult, OP.add))
        self.mark(f"slabsel_{hh}", self.sDVE, v)
        self.wait(e, en, f"slabdma_{hh}")
        e.tensor_scalar(osl[:, :], q[:, 0:1024], self.selA_sb[:, 0:1],
                        self.bv_sb[:, hh:hh + 1], OP.mult, OP.add)
        v = self.sDVE.inc(
            e.scalar_tensor_tensor(osl[:, :], q[:, 1024:2048],
                                   self.selB_sb[:, 0:1], osl[:, :],
                                   OP.mult, OP.add))
        self.mark(f"ownsel_{hh}", self.sDVE, v)

    @self.emit("gpsimd")
    def _(e, en, hh=hh):
        self.wait(e, en, f"slabsel_{hh}")
        self.dmae(e, en, f"sSt{hh%2}",
                  self.cc_in[hh * 128:(hh + 1) * 128, :],
                  self.o_sel[hh][:, :], mark=f"slabdma_{hh}")


def ph4_exchange(self):
    @self.emit("gpsimd")
    def _(e, en):
        for hh in range(HQC):
            self.wait(e, en, f"slabdma_{hh}")
        v = self.sCC.inc(e.collective_compute(
            "AllGather", OP.bypass,
            replica_groups=[list(range(8))],
            ins=[self.cc_in[:, :]],
            outs=[self.cc_out[:, :]]))
        self.mark("cc", self.sCC, v)
        self.wait(e, en, "cc")
        for j in range(16):
            v = self.sCC.inc(e.indirect_dma_start(
                out=self.o_oth[j][:, :],
                out_offset=None,
                in_=self.cc_out[0:128, :],
                in_offset=IndirectOffsetOnAxis(self.pidx_sb[:, j:j + 1],
                                               axis=0)), 16)
            self.mark(f"ooth_{j}", self.sCC, v)


def ph5_outproj(self):
    for oc in range(32):
        sli = oc % 2

        @self.emit("gpsimd")
        def _(e, en, oc=oc, sli=sli):
            if oc < 2:
                self.wait(e, en, f"ownsel_{oc}")
            if oc >= 2:
                self.wait(e, en, f"opmm_{oc-2}_1")
            self.dmae(e, en, f"sW{oc%2}",
                      self.wosl[sli][:, :].rearrange("p (kc m) -> p kc m",
                                                     kc=KC),
                      self.hwo[oc, :, :, :], mark=f"wo_{oc}")

        @self.emit("tensor")
        def _(e, en, oc=oc, sli=sli):
            self.wait(e, en, f"wo_{oc}")
            if oc == 0:
                self.wait(e, en, f"ownsel_{HQC-1}")
                self.wait(e, en, f"ooth_{15}")
            pa, pb = ((self.psA, self.psB) if oc % 2 == 0
                      else (self.psC, self.psD))
            if oc >= 2:
                self.wait(e, en, f"uevac_{oc-2}_1")
            ins = None
            for dc in range(32):
                rhs = (self.o_sel[dc] if dc < 16 else self.o_oth[dc - 16])
                e.matmul(pa[:, :],
                         self.wosl[sli][:, dc * 128:(dc + 1) * 128],
                         rhs[:, 0:512], start=(dc == 0), stop=(dc == 31))
                ins = e.matmul(pb[:, :],
                               self.wosl[sli][:, dc * 128:(dc + 1) * 128],
                               rhs[:, 512:1024],
                               start=(dc == 0), stop=(dc == 31))
            v = self.sPE.inc(ins)
            self.mark(f"opmm_{oc}_1", self.sPE, v)

        for g in range(2):
            @self.emit("scalar")
            def _(e, en, oc=oc, g=g):
                self.wait(e, en, f"opmm_{oc}_1")
                pa, pb = ((self.psA, self.psB) if oc % 2 == 0
                          else (self.psC, self.psD))
                ps = pa if g == 0 else pb
                uv = self.u_all[:, :].rearrange(
                    "p (ti oc t) -> p ti oc t", ti=8, oc=32)
                v = self.sACT.inc(e.activation(
                    uv[:, g * 4:(g + 1) * 4, oc, :],
                    ps[:, :].rearrange("p (ti t) -> p ti t", ti=4),
                    AF.Identity, bias=self.bo_sb[:, oc:oc + 1]))
                self.mark(f"uevac_{oc}_{g}", self.sACT, v)


def dump_u(self):
    du = self.dumper("u", [128, 8 * 32 * 128], F32)

    @self.emit("gpsimd")
    def _(e, en):
        for oc in range(32):
            self.wait(e, en, f"uevac_{oc}_1")
        v = self.sSt2.inc(e.dma_start(out=du[:, :], in_=self.u_all[:, :]),
                          16)
        self.mark("dump_u", self.sSt2, v)


def ph6_ffn(self):
    for ti in range(8):
        @self.emit("sync")
        def _(e, en, ti=ti):
            for oc in range(32):
                self.wait(e, en, f"uevac_{oc}_{ti//4}")
            dst = self.u_tm[ti][:, :].rearrange("p (oc t) -> p oc t", oc=32)
            self.dmae(e, en, f"sTp{ti%2}", dst,
                      self.u_all[:, ti * D:(ti + 1) * D], mark=f"utr_{ti}",
                      transpose=True)

    @self.emit("gpsimd")
    def _(e, en):
        self.wait(e, en, "utr_7")
        self.dmae(e, en, "sW4",
                  self.la_sb[:, :].rearrange("p (kc r) -> p kc r", kc=KC),
                  self.hla[:, :, :], mark="la")
        self.dmae(e, en, "sW4", self.lb_sb[:, :], self.hlb[:, :], mark="lb")

    for g in range(2):
        self._ph6_stage1b(g)
        self._ph6_stage2(g)
        self._ph6_stage3(g)


def _ln_stats(self, e, src, ti, pfx):
    ln = self.lnst
    v = None
    sA = self.sems["sACT"]
    for i in range(4):
        sl = src[:, i * 1024:(i + 1) * 1024]
        if v is not None:
            self.wait_val(e, "scalar", sA, v)
        e.activation(self.ps4[:, 0:1024], sl, AF.Square,
                     accum_out=ln[:, 40 + i:41 + i])
        v = self.sACT.inc(
            e.activation(self.ps4[:, 1024:2048], sl, AF.Copy,
                         accum_out=ln[:, 44 + i:45 + i]))
    self.mark(f"{pfx}_sums_{ti}", self.sACT, v)


def _ln_finish(self, e, pfx, ti):
    ln = self.lnst
    e.tensor_tensor(ln[:, 48:49], ln[:, 40:41], ln[:, 41:42], OP.add)
    e.tensor_tensor(ln[:, 49:50], ln[:, 42:43], ln[:, 43:44], OP.add)
    e.tensor_tensor(ln[:, 50:51], ln[:, 44:45], ln[:, 45:46], OP.add)
    e.tensor_tensor(ln[:, 51:52], ln[:, 46:47], ln[:, 47:48], OP.add)
    e.tensor_tensor(ln[:, 48:49], ln[:, 48:49], ln[:, 49:50], OP.add)
    e.tensor_tensor(ln[:, 50:51], ln[:, 50:51], ln[:, 51:52], OP.add)
    e.tensor_scalar(ln[:, 52:53], ln[:, 50:51], 1.0 / D, None, OP.mult)
    e.tensor_scalar(ln[:, 53:54], ln[:, 48:49], 1.0 / D, None, OP.mult)
    e.tensor_tensor(ln[:, 54:55], ln[:, 52:53], ln[:, 52:53], OP.mult)
    e.tensor_tensor(ln[:, 53:54], ln[:, 53:54], ln[:, 54:55], OP.subtract)
    v = self.sDVE.inc(
        e.tensor_scalar(ln[:, 53:54], ln[:, 53:54], 1.0, 1e-5,
                        OP.mult, OP.add))
    self.mark(f"{pfx}_var_{ti}", self.sDVE, v)


def _ln_rb(self, e, en, rcol, cbcol, pfx, ti):
    ln = self.lnst
    self.wait(e, en, f"{pfx}_sd_{ti}")
    e.reciprocal(ln[:, rcol:rcol + 1], ln[:, 55:56])
    e.tensor_copy(self.junk[:, 0:1], ln[:, 52:53])
    e.tensor_tensor(ln[:, cbcol:cbcol + 1], ln[:, 52:53],
                    ln[:, rcol:rcol + 1], OP.mult)
    v = self.sDVE.inc(
        e.tensor_scalar(ln[:, cbcol:cbcol + 1], ln[:, cbcol:cbcol + 1],
                        -1.0, None, OP.mult))
    self.mark(f"{pfx}_rb_{ti}", self.sDVE, v)


def _ph6_stage1b(self, g):
    for tt in range(4):
        ti = g * 4 + tt
        r2c, cb2c = ti * 4, ti * 4 + 1

        @self.emit("sync")
        def _(e, en, ti=ti, g=g, tt=tt):
            if ti == 0:
                self.wait(e, en, "uevac_31_1")
            if ti > 0:
                self.wait(e, en, f"x1add_{ti-1}")
            if g == 1 and tt == 0:
                self.wait(e, en, f"x2_{3}")
                self.wait(e, en, f"store_{3}")
            tok0 = g * 512 + tt * 128
            self.dmae(e, en, "sXa", self.xobuf[:, :],
                      self.hxo[tok0:tok0 + 128, :], mark=f"xoload_{ti}")

        @self.emit("scalar")
        def _(e, en, ti=ti, g=g, tt=tt, r2c=r2c, cb2c=cb2c):
            self.wait(e, en, f"utr_{ti}")
            if ti > 0:
                self.wait(e, en, f"ln3ap_{ti-1}")
            if g == 1 and tt == 0:
                self.wait(e, en, f"x2_{3}")
            _ln_stats(self, e, self.u_tm[ti], ti, "ln2")

        @self.emit("vector")
        def _(e, en, ti=ti):
            self.wait(e, en, f"ln2_sums_{ti}")
            _ln_finish(self, e, "ln2", ti)

        @self.emit("scalar")
        def _(e, en, ti=ti):
            self.wait(e, en, f"ln2_var_{ti}")
            v = self.sACT.inc(
                e.activation(self.lnst[:, 55:56], self.lnst[:, 53:54],
                             AF.Sqrt))
            self.mark(f"ln2_sd_{ti}", self.sACT, v)

        @self.emit("scalar")
        def _(e, en, ti=ti, r2c=r2c, cb2c=cb2c):
            self.wait(e, en, f"ln2_rb_{ti}")
            v = self.sACT.inc(
                e.activation(self.x1buf[:, :], self.u_tm[ti][:, :],
                             AF.Identity,
                             scale=self.lnst[:, r2c:r2c + 1],
                             bias=self.lnst[:, cb2c:cb2c + 1]))
            self.mark(f"ln2ap_{ti}", self.sACT, v)

        @self.emit("vector")
        def _(e, en, ti=ti, r2c=r2c, cb2c=cb2c):
            _ln_rb(self, e, en, r2c, cb2c, "ln2", ti)
            self.wait(e, en, f"ln2ap_{ti}")
            self.wait(e, en, f"xoload_{ti}")
            v = self.sDVE.inc(
                e.tensor_tensor(self.x1buf[:, :], self.x1buf[:, :],
                                self.xobuf[:, :], OP.add))
            self.mark(f"x1add_{ti}", self.sDVE, v)

        @self.emit("scalar")
        def _(e, en, ti=ti):
            self.wait(e, en, f"x1add_{ti}")
            _ln_stats(self, e, self.x1buf, ti, "ln3")

        @self.emit("vector")
        def _(e, en, ti=ti):
            self.wait(e, en, f"ln3_sums_{ti}")
            _ln_finish(self, e, "ln3", ti)

        @self.emit("scalar")
        def _(e, en, ti=ti):
            self.wait(e, en, f"ln3_var_{ti}")
            v = self.sACT.inc(
                e.activation(self.lnst[:, 55:56], self.lnst[:, 53:54],
                             AF.Sqrt))
            self.mark(f"ln3_sd_{ti}", self.sACT, v)

        @self.emit("vector")
        def _(e, en, ti=ti):
            _ln_rb(self, e, en, 56, 57, "ln3", ti)

        @self.emit("scalar")
        def _(e, en, ti=ti, g=g, tt=tt):
            self.wait(e, en, f"ln3_rb_{ti}")
            v = self.sACT.inc(
                e.activation(self.xstage[:, :], self.x1buf[:, :], AF.Identity,
                             scale=self.lnst[:, 56:57],
                             bias=self.lnst[:, 57:58]))
            self.mark(f"ln3ap_{ti}", self.sACT, v)

        @self.emit("sync")
        def _(e, en, ti=ti, g=g, tt=tt):
            self.wait(e, en, f"ln3ap_{ti}")
            self.wait(e, en, f"utr_{g*4+3}")
            if g == 1 and tt == 0:
                self.wait(e, en, f"ffnmm_last_{g-1}")
            dst = self.xln3_sb[:, :].rearrange("p (kc t) -> p kc t", kc=KC)
            self.dmae(e, en, f"sT{g%2}",
                      dst[:, :, tt * 128:(tt + 1) * 128],
                      self.xstage[:, :], mark=f"xln3tr_{ti}", transpose=True,
                      selfgate=False)


def _ph6_stage2(self, g):
    def xl(kc, c0, cn):
        return self.xln3_sb[:, kc * 512 + c0:kc * 512 + c0 + cn]

    @self.emit("tensor")
    def _(e, en, g=g):
        self.wait(e, en, f"xln3tr_{g*4+3}")
        self.wait(e, en, "gw")
        if g == 0:
            self.wait(e, en, "uevac_31_1")
        else:
            self.wait(e, en, f"ffnevac_{g-1}_31")
            self.wait(e, en, f"smatevac_{g-1}")
            self.wait(e, en, f"lowc_{g-1}")
        ins = None
        for kc in range(KC):
            for tt in range(4):
                ins = e.matmul(
                    self.psC[:, tt * 8:(tt + 1) * 8],
                    xl(kc, tt * 128, 128),
                    self.gw_sb[:, kc * N_EXP:(kc + 1) * N_EXP],
                    start=(kc == 0 and tt == 0), stop=(kc == KC - 1),
                    skip_group_check=True)
        v = self.sPE.inc(ins)
        self.mark(f"gatemm_{g}", self.sPE, v)

    @self.emit("scalar")
    def _(e, en, g=g):
        self.wait(e, en, f"gatemm_{g}")
        v = self.sACT.inc(e.activation(self.gsc[:, 0:32], self.psC[:, 0:32],
                                       AF.Copy))
        self.mark(f"gevac_{g}", self.sACT, v)

    for tt in range(4):
        gb = 32 + tt * 8
        wfb = 64 + tt * 8
        inb = 96 + tt * 8

        @self.emit("vector")
        def _(e, en, g=g, tt=tt, gb=gb, wfb=wfb, inb=inb):
            gs = self.gsc
            gv = gs[:, tt * 8:(tt + 1) * 8]
            self.wait(e, en, f"gevac_{g}")
            e.tensor_reduce(gs[:, gb:gb + 1], gv, AXX, OP.max, negate=True)
            e.tensor_copy(self.junk[:, 0:1], gs[:, gb:gb + 1])
            e.tensor_scalar(gs[:, inb:inb + 8], gv, gs[:, gb:gb + 1], None,
                            OP.add)
            e.tensor_single_scalar(gs[:, inb:inb + 8], gs[:, inb:inb + 8],
                                   0.0, OP.is_lt)
            e.tensor_scalar(gs[:, inb:inb + 8], gs[:, inb:inb + 8],
                            1.0e30, -1.0e30, OP.mult, OP.add)
            e.tensor_tensor(gs[:, inb:inb + 8], gv, gs[:, inb:inb + 8],
                            OP.add)
            e.tensor_reduce(gs[:, gb + 1:gb + 2], gs[:, inb:inb + 8], AXX,
                            OP.max, negate=True)
            e.tensor_copy(self.junk[:, 0:1], gs[:, gb:gb + 1])
            e.tensor_tensor(gs[:, gb + 2:gb + 3], gs[:, gb:gb + 1],
                            gs[:, gb + 1:gb + 2], OP.subtract)
            v = self.sDVE.inc(
                e.tensor_scalar(gs[:, inb:inb + 8], gv,
                                gs[:, gb + 1:gb + 2], 0.0, OP.add, OP.is_ge))
            self.mark(f"top2a_{g}_{tt}", self.sDVE, v)

        @self.emit("scalar")
        def _(e, en, g=g, tt=tt, gb=gb, wfb=wfb):
            gs = self.gsc
            self.wait(e, en, f"top2a_{g}_{tt}")
            e.activation(gs[:, gb + 3:gb + 4], gs[:, gb + 2:gb + 3], AF.Exp)
            v = self.sACT.inc(
                e.activation(gs[:, wfb:wfb + 8], gs[:, tt * 8:(tt + 1) * 8],
                             AF.Exp, bias=gs[:, gb:gb + 1]))
            self.mark(f"top2b_{g}_{tt}", self.sACT, v)

        @self.emit("vector")
        def _(e, en, g=g, tt=tt, gb=gb, wfb=wfb, inb=inb):
            gs = self.gsc
            self.wait(e, en, f"top2b_{g}_{tt}")
            e.tensor_scalar(gs[:, gb + 4:gb + 5], gs[:, gb + 3:gb + 4],
                            1.0, 1.0, OP.mult, OP.add)
            e.tensor_copy(self.junk[:, 0:1], gs[:, gb:gb + 1])
            e.reciprocal(gs[:, gb + 5:gb + 6], gs[:, gb + 4:gb + 5])
            e.tensor_tensor(gs[:, wfb:wfb + 8], gs[:, wfb:wfb + 8],
                            gs[:, inb:inb + 8], OP.mult)
            e.tensor_copy(self.junk[:, 0:1], gs[:, gb:gb + 1])
            v = self.sDVE.inc(
                e.tensor_scalar(gs[:, wfb:wfb + 8], gs[:, wfb:wfb + 8],
                                gs[:, gb + 5:gb + 6], None, OP.mult))
            self.mark(f"wf_{g}_{tt}", self.sDVE, v)

        @self.emit("scalar")
        def _(e, en, g=g, tt=tt, wfb=wfb):
            self.wait(e, en, f"wf_{g}_{tt}")
            v = self.sACT.inc(e.activation(self.wfbf_sb[:, :],
                                           self.gsc[:, wfb:wfb + 8], AF.Copy))
            self.mark(f"wfbf_{g}_{tt}", self.sACT, v)

        @self.emit("tensor")
        def _(e, en, g=g, tt=tt):
            self.wait(e, en, f"wfbf_{g}_{tt}")
            self.wait(e, en, "consts")
            if tt > 0:
                self.wait(e, en, f"wftevac_{g}_{tt-1}")
            elif g > 0:
                self.wait(e, en, f"wftevac_{g-1}_3")
                self.wait(e, en, f"lowc_{g-1}")
            dst = self.psD[0:8, tt * 64:tt * 64 + 64].bitcast(BF16)
            v = self.sPE.inc(e.transpose(dst, self.wfbf_sb[:, :],
                                         self.ident_sb[:, :]))
            self.mark(f"wft_{g}_{tt}", self.sPE, v)

        @self.emit("scalar")
        def _(e, en, g=g, tt=tt):
            self.wait(e, en, f"wft_{g}_{tt}")
            src = self.psD[0:8, tt * 64:tt * 64 + 64].bitcast(BF16)
            v = self.sACT.inc(e.activation(
                self.wft_sb[0:8, tt * 128:(tt + 1) * 128], src, AF.Copy))
            self.mark(f"wftevac_{g}_{tt}", self.sACT, v)

    @self.emit("tensor")
    def _(e, en, g=g):
        self.wait(e, en, f"wftevac_{g}_3")
        self.wait(e, en, f"gevac_{g}")
        v = self.sPE.inc(e.matmul(self.psC[:, :], self.rexp_sb[:, :],
                                  self.wft_sb[0:8, :], start=True, stop=True))
        self.mark(f"smatmm_{g}", self.sPE, v)

    @self.emit("scalar")
    def _(e, en, g=g):
        self.wait(e, en, f"smatmm_{g}")
        v = self.sACT.inc(e.activation(self.smat_sb[:, :], self.psC[:, :],
                                       AF.Copy))
        self.mark(f"smatevac_{g}", self.sACT, v)

    @self.emit("tensor")
    def _(e, en, g=g):
        self.wait(e, en, "la")
        self.wait(e, en, f"wftevac_{g}_3")
        if g > 0:
            self.wait(e, en, f"lowc_{g-1}")
        ins = None
        for kc in range(KC):
            ins = e.matmul(self.psD[:, :],
                           self.la_sb[:, kc * 128:(kc + 1) * 128],
                           xl(kc, 0, 512), start=(kc == 0),
                           stop=(kc == KC - 1))
        v = self.sPE.inc(ins)
        self.mark(f"lowmm_{g}", self.sPE, v)

    @self.emit("vector")
    def _(e, en, g=g):
        self.wait(e, en, f"lowmm_{g}")
        self.wait(e, en, f"smatevac_{g}")
        v = self.sDVE.inc(
            e.tensor_tensor(self.low_sb[:, :], self.psD[:, :],
                            self.smat_sb[:, :], OP.mult))
        self.mark(f"lowc_{g}", self.sDVE, v)

    for oc in range(32):
        sli = oc % 2

        @self.emit("gpsimd")
        def _(e, en, g=g, oc=oc, sli=sli):
            gi = g * 32 + oc
            if gi >= 2:
                self.wait(e, en, f"ffnmm_{(gi-2)//32}_{(gi-2)%32}")
            if oc < 2:
                self.wait(e, en, f"ln3ap_{g*4+3}")
                if g == 1:
                    self.wait(e, en, f"x2_{3}")
                    self.wait(e, en, f"store_{3}")
            gi2 = g * 32 + oc
            self.dmae(e, en, f"sW{gi2%2}",
                      self.bwsl[sli][:, :].rearrange("p (kc m) -> p kc m",
                                                     kc=KC),
                      self.hbw[oc, :, :, :], mark=f"bw_{g}_{oc}")

        @self.emit("tensor")
        def _(e, en, g=g, oc=oc, sli=sli):
            self.wait(e, en, f"bw_{g}_{oc}")
            self.wait(e, en, f"lowc_{g}")
            ps = self.psA if oc % 2 == 0 else self.psB
            if oc >= 2:
                self.wait(e, en, f"ffnevac_{g}_{oc-2}")
            elif g == 0:
                self.wait(e, en, "uevac_31_1")
            else:
                self.wait(e, en, f"ffnevac_{g-1}_31")
            for kc in range(KC):
                e.matmul(ps[:, :], self.bwsl[sli][:, kc * 128:(kc + 1) * 128],
                         xl(kc, 0, 512), start=(kc == 0), stop=False)
            ins = e.matmul(ps[:, :], self.lb_sb[:, oc * 128:(oc + 1) * 128],
                           self.low_sb[:, :], start=False, stop=True)
            v = self.sPE.inc(ins)
            self.mark(f"ffnmm_{g}_{oc}", self.sPE, v)
            if oc == 31:
                self.mark(f"ffnmm_last_{g}", self.sPE, v)

        @self.emit("scalar")
        def _(e, en, g=g, oc=oc):
            self.wait(e, en, f"ffnmm_{g}_{oc}")
            ps = self.psA if oc % 2 == 0 else self.psB
            fv = self.ffn_all[:, :].rearrange("p (ti oc t) -> p ti oc t",
                                              ti=4, oc=32)
            v = self.sACT.inc(e.activation(
                fv[:, :, oc, :],
                ps[:, :].rearrange("p (ti t) -> p ti t", ti=4),
                AF.Identity, bias=self.bb_sb[:, oc:oc + 1]))
            self.mark(f"ffnevac_{g}_{oc}", self.sACT, v)


def _ph6_stage3(self, g):
    for tt in range(4):
        ti = g * 4 + tt
        r2c, cb2c = ti * 4, ti * 4 + 1
        r4c, cb4c = ti * 4 + 2, ti * 4 + 3

        @self.emit("sync")
        def _(e, en, g=g, tt=tt, ti=ti):
            self.wait(e, en, f"ffnevac_{g}_31")
            if ti >= 2:
                self.wait(e, en, f"x2_{ti-2}")
            dst = self.ffn_tm[tt % 2][:, :].rearrange("p (oc t) -> p oc t",
                                                      oc=32)
            self.dmae(e, en, f"sTo{ti%2}", dst,
                      self.ffn_all[:, tt * D:(tt + 1) * D],
                      mark=f"ffntr_{ti}", transpose=True)

        @self.emit("scalar")
        def _(e, en, g=g, tt=tt, ti=ti):
            self.wait(e, en, f"ffntr_{ti}")
            if ti > 0:
                self.wait(e, en, f"x2_{ti-1}")
            _ln_stats(self, e, self.ffn_tm[tt % 2], ti, "ln4")

        @self.emit("vector")
        def _(e, en, ti=ti):
            self.wait(e, en, f"ln4_sums_{ti}")
            _ln_finish(self, e, "ln4", ti)

        @self.emit("scalar")
        def _(e, en, ti=ti):
            self.wait(e, en, f"ln4_var_{ti}")
            v = self.sACT.inc(
                e.activation(self.lnst[:, 55:56], self.lnst[:, 53:54],
                             AF.Sqrt))
            self.mark(f"ln4_sd_{ti}", self.sACT, v)

        @self.emit("vector")
        def _(e, en, ti=ti, r4c=r4c, cb4c=cb4c):
            _ln_rb(self, e, en, r4c, cb4c, "ln4", ti)

        @self.emit("sync")
        def _(e, en, ti=ti, g=g, tt=tt):
            self.wait(e, en, f"x1add_{g*4+3}")
            self.wait(e, en, f"ffnmm_last_{g}")
            if ti > 0:
                self.wait(e, en, f"store_{ti-1}")
            tok0 = g * 512 + tt * 128
            self.dmae(e, en, "sXa", self.xobuf[:, :],
                      self.hxo[tok0:tok0 + 128, :], mark=f"xoload2_{ti}")

        @self.emit("vector")
        def _(e, en, ti=ti, g=g, tt=tt, r2c=r2c, cb2c=cb2c, r4c=r4c,
              cb4c=cb4c):
            ln = self.lnst
            self.wait(e, en, f"ln4_rb_{ti}")
            self.wait(e, en, f"ln3ap_{g*4+3}")
            e.tensor_tensor(ln[:, 58:59], ln[:, cb2c:cb2c + 1],
                            ln[:, cb4c:cb4c + 1], OP.add)
            e.tensor_copy(self.junk[:, 0:1], ln[:, 58:59])
            e.tensor_scalar(self.x1buf[:, :], self.u_tm[ti][:, :],
                            ln[:, r2c:r2c + 1], ln[:, 58:59],
                            OP.mult, OP.add)
            self.wait(e, en, f"xoload2_{ti}")
            e.tensor_tensor(self.x1buf[:, :], self.x1buf[:, :],
                            self.xobuf[:, :], OP.add)
            self.wait(e, en, f"ffntr_{ti}")
            v = self.sDVE.inc(
                e.scalar_tensor_tensor(self.xobuf[:, :],
                                       self.ffn_tm[tt % 2][:, :],
                                       ln[:, r4c:r4c + 1],
                                       self.x1buf[:, :], OP.mult, OP.add))
            self.mark(f"x2_{ti}", self.sDVE, v)

        @self.emit("sync")
        def _(e, en, ti=ti, g=g, tt=tt):
            self.wait(e, en, f"x2_{ti}")
            tok0 = g * 512 + tt * 128
            s2 = self.sems["sSt2"]
            v = self.dmae(e, en, "sSt2", self.hout[tok0:tok0 + 128, :],
                          self.xobuf[:, :], mark=f"store_{ti}")
            if ti == 7:
                self.mark("stores_done", s2, v)


def dump_ffn(self):
    due = self.dumper("utme", [8, 128, 4096])

    @self.emit("gpsimd")
    def _(e, en):
        self.wait(e, en, "utr_7")
        s2 = self.sems["sSt2"]
        v = None
        for t in range(8):
            v = s2.inc(e.dma_start(out=due[t, :, :],
                                   in_=self.u_tm[t][:, :]), 16)
        self.mark("dump_utme", s2, v)

    du = self.dumper("utm", [4, 128, 4096])
    dx3 = self.dumper("xln3", [128, KC * 512])
    dfa = self.dumper("ffa", [128, 4 * 32 * 128])
    dln = self.dumper("lnst", [128, 64])
    dlow = self.dumper("low", [128, 512])
    dsm = self.dumper("smat", [128, 512])
    dgs = self.dumper("gsc", [128, 128])

    @self.emit("gpsimd")
    def _(e, en):
        self.wait(e, en, "stores_done")
        s2 = self.sems["sSt2"]
        for t in range(4):
            s2.inc(e.dma_start(out=du[t, :, :], in_=self.u_tm[4 + t][:, :]),
                   16)
        s2.inc(e.dma_start(out=dx3[:, :], in_=self.xln3_sb[:, :]), 16)
        s2.inc(e.dma_start(out=dfa[:, :], in_=self.ffn_all[:, :]), 16)
        s2.inc(e.dma_start(out=dln[:, :], in_=self.lnst[:, :]), 16)
        s2.inc(e.dma_start(out=dlow[:, :], in_=self.low_sb[:, :]), 16)
        s2.inc(e.dma_start(out=dsm[:, :], in_=self.smat_sb[:, :]), 16)
        v = s2.inc(e.dma_start(out=dgs[:, :], in_=self.gsc[:, :]), 16)
        self.wait(e, en, "dump_utme")
        self.mark("dump_ffn", s2, v)


def ph_final(self):
    finals = []
    if self._on("p6"):
        finals.append("stores_done")
    for nm, d in (("dump_qkv", "qkv"), ("dump_o", "o"), ("dump_u", "u"),
                  ("dump_ffn", "ffn")):
        if d in self.dumps:
            finals.append(nm)
    if not finals:
        finals.append("consts")

    for ename in ["sync", "gpsimd"]:
        @self.emit(ename)
        def _(e, en):
            for f in finals:
                self.wait(e, en, f)


for _nm in ["ph3_attention", "dump_o", "ph4_head", "ph4_exchange",
            "ph5_outproj", "dump_u", "ph6_ffn", "dump_ffn",
            "_ph6_stage1b", "_ph6_stage2", "_ph6_stage3", "ph_final"]:
    setattr(MK, _nm, globals()[_nm])


_CACHED = {}


def _get_program():
    if "nc" not in _CACHED:
        nc, mk = build_kernel(upto="p6", dumps=())
        _CACHED["nc"] = nc
        _CACHED["mk"] = mk
    return _CACHED["nc"]


def kernel(**inputs):
    from concourse.bass_utils import run_bass_kernel_spmd
    nc = _get_program()
    in_maps = [make_core_inputs(inputs, c) for c in range(8)]
    res = run_bass_kernel_spmd(nc, in_maps, core_ids=list(range(8)))
    out = np.empty((B, S, D), dtype=np.float32)
    for c in range(8):
        bb, h = c // 2, c % 2
        out[bb, h * TOK:(h + 1) * TOK] = res.results[c]["out"]
    return out
